# version 16
# baseline (speedup 1.0000x reference)
import numpy as np
import ml_dtypes
import concourse.bass as bass
import concourse.mybir as mybir
from concourse.alu_op_type import AluOpType
from concourse.bass_utils import run_bass_kernel_spmd

F32 = mybir.dt.float32
BF16 = mybir.dt.bfloat16
AF = mybir.ActivationFunctionType
OP = AluOpType

ENGINES = ("sync", "tensor", "vector", "scalar", "gpsimd")

D = 1024
H = 16
DH = 64
FF = 4096
T = 2048
SEQS = 2
TOK = SEQS * T
CH = 128
SCT = 512
NSC = TOK // SCT
CPS = SCT // CH
KC = D // 128
HC = FF // 128
LN_EPS = 1e-5
MIX_EPS = 1e-6
ABS_EPS = 1e-6


class Buf:
    __slots__ = ("w", "r")

    def __init__(self, init=()):
        self.w = list(init)
        self.r = []


NDMA = 8


class Prog:
    def __init__(self):
        self.semnames = list(ENGINES) + [f"sd{i}" for i in range(NDMA)] \
            + [f"gd{i}" for i in range(NDMA)]
        self.ops = {e: [] for e in ENGINES}
        self.count = {s: 0 for s in self.semnames}
        self.dma_idx = {"sync": 0, "gpsimd": 0}

    def add(self, eng, fn, reads=(), writes=(), deps=(), inc=1, sem=None):
        sem = sem or eng
        dep = list(deps)
        for b in reads:
            dep += b.w
        for b in writes:
            dep += b.r
            dep += b.w
        need = {}
        for (ds, dv) in dep:
            assert dv <= self.count[ds], "dependency on future op"
            need[ds] = max(need.get(ds, 0), dv)
        self.ops[eng].append((fn, inc, sem, need))
        self.count[sem] += inc
        h = (sem, self.count[sem])
        for b in reads:
            b.r.append(h)
        for b in writes:
            b.w = [h]
            b.r = []
        return h

    def add_dma(self, eng, fn, reads=(), writes=(), deps=()):
        i = self.dma_idx[eng] % NDMA
        self.dma_idx[eng] += 1
        sem = ("sd" if eng == "sync" else "gd") + str(i)
        deps = list(deps) + [(sem, self.count[sem])]
        return self.add(eng, fn, reads=reads, writes=writes, deps=deps, inc=16, sem=sem)

    def barrier_handles(self):
        return [(s, self.count[s]) for s in self.semnames]

    def emit(self, block, sems):
        for eng in ENGINES:
            def make(eng):
                def body(e):
                    water = {k: 0 for k in self.semnames}
                    for fn, inc, sem, need in self.ops[eng]:
                        for ds, dv in need.items():
                            if dv > water[ds]:
                                e.wait_ge(sems[ds], dv)
                                water[ds] = dv
                        ins = fn(e)
                        ins.then_inc(sems[sem], inc)
                        if sem == eng:
                            water[sem] = max(water[sem], self.count_at_emit(eng, sem))
                    if eng == "gpsimd":
                        for s in ["gpsimd"] + [f"gd{i}" for i in range(NDMA)]:
                            if self.count[s] > water[s]:
                                e.wait_ge(sems[s], self.count[s])
                return body
            getattr(block, eng)(make(eng))

    def count_at_emit(self, eng, sem):
        return 0


def build_core_program(debug=False, n_sc=NSC, phase_b=True, stages=None):
    if stages is None:
        stages = {"qk", "ktr", "v", "st", "num", "kv", "yt", "proj", "ln1", "x2t", "tasks"}
    ON = stages.__contains__
    nc = bass.Bass("TRN2", target_bir_lowering=False)

    x_fm_d = nc.dram_tensor("x_fm", [SEQS * D, T], F32, kind="ExternalInput").ap()
    x_tm_d = nc.dram_tensor("x_tm", [TOK, D], F32, kind="ExternalInput").ap()
    wattn_d = nc.dram_tensor("wattn", [D, 3 * D], F32, kind="ExternalInput").ap()
    bqk_d = nc.dram_tensor("bqk", [128, 16], F32, kind="ExternalInput").ap()
    bv_d = nc.dram_tensor("bv", [1, D], F32, kind="ExternalInput").ap()
    bproj_d = nc.dram_tensor("bproj", [1, D], F32, kind="ExternalInput").ap()
    wproj_d = nc.dram_tensor("wproj", [D, D], F32, kind="ExternalInput").ap()
    wfct_d = nc.dram_tensor("wfct", [D, FF], F32, kind="ExternalInput").ap()
    wpft_d = nc.dram_tensor("wpft", [FF, D], F32, kind="ExternalInput").ap()
    lnw_d = nc.dram_tensor("lnw", [128, HC], F32, kind="ExternalInput").ap()
    lnb_d = nc.dram_tensor("lnb", [128, HC], F32, kind="ExternalInput").ap()
    out_d = nc.dram_tensor("out", [TOK, D], F32, kind="ExternalOutput").ap()
    kind_dbg = "ExternalOutput" if debug else "Internal"
    x2tm_s = nc.dram_tensor("x2tm_s", [TOK, D], F32, kind=kind_dbg).ap()
    x2fm_s = nc.dram_tensor("x2fm_s", [SEQS * D, T], BF16, kind="Internal").ap()
    wqp_s = nc.dram_tensor("wqp_s", [FF, D], BF16, kind=kind_dbg).ap()
    t3_s = nc.dram_tensor("t3_s", [TOK, D], F32, kind="Internal").ap()

    from contextlib import ExitStack
    with ExitStack() as stack:
        E = stack.enter_context
        AB = E(nc.sbuf_tensor("arena_b", [128, 93528], BF16))
        AFR = E(nc.sbuf_tensor("arena_f", [128, 6296], F32))
        ps_qk0 = E(nc.psum_tensor("ps_qk0", [128, 512], F32))
        ps_qk1 = E(nc.psum_tensor("ps_qk1", [128, 512], F32))
        ps_wide = E(nc.psum_tensor("ps_wide", [128, 1024], F32))
        ps_c = E(nc.psum_tensor("ps_c", [128, 512], F32))
        ps_d = E(nc.psum_tensor("ps_d", [128, 512], F32))
        ps_e = E(nc.psum_tensor("ps_e", [128, 1024], BF16))
        ps_7 = E(nc.psum_tensor("ps_7", [128, 512], F32))
        semnames = list(ENGINES) + [f"sd{i}" for i in range(8)] + [f"gd{i}" for i in range(8)]
        sems = {s: E(nc.semaphore(f"sem_{s}")) for s in semnames}
        block = E(nc.Block())

        P = Prog()

        WQFC_O = 0
        ROWP_O = 32768
        BVROW_O = ROWP_O
        BPROW_O = ROWP_O
        UROW_O = ROWP_O
        CROW_O = 33792
        ONEROW_O = 34816
        IDENTB_O = 34944
        ONECOL_O = 35072
        A0 = 35088
        WQK_O = A0
        WV_O = WQK_O + 16384
        WPROJ_O = WV_O + 8192
        XFM_O = WPROJ_O + 8192
        QFM_O = XFM_O + 8192
        KFM_O = QFM_O + 4096
        VTM_O = KFM_O + 4096
        KTM_O = VTM_O + 1024
        AT_O = KTM_O + 1024
        YTM_O = AT_O + 1024
        YFM_O = YTM_O + 1024
        X2FMB_O = YFM_O + 1024
        WQSTG_O = X2FMB_O + 1024
        BSTG_O = WQSTG_O + 512
        ESTG_O = BSTG_O + 512
        KVB_O = ESTG_O + 1024
        MASK4B_O = KVB_O + 520
        WBBF2_O = MASK4B_O + 512
        ABSZ = WBBF2_O + 64
        WQPT_O = A0
        HFM_O = WQPT_O + 32768
        X2FMBB_O = HFM_O + 16384
        WQPSTG_O = X2FMBB_O + 4096
        WQPSTG2_O = WQPSTG_O + 1024
        assert WQPSTG2_O + 1024 <= ABSZ

        MASKF_O = 0
        IDENTF_O = 128
        BQK_O = 256
        BQK1_O = 272
        LNW_O = 288
        LNB_O = 320
        GACC_O = 352
        THRP_O = 416
        GSUM_O = 420
        BNS_O = 432
        AGG_O = 444
        VB_O = 446
        RB_O = 454
        DSTG_O = 462
        RECS_O = 466
        NM_O = 482
        ONECF_O = 486
        R1_O = 487
        MU2_O = 489
        NM2_O = 493
        SS_O = 497
        CONSTS_O = 501
        ONEROWF_O = 512
        AF0 = 656
        XFMSTG_O = AF0
        XFMSTG2_O = AF0 + 512
        XTM_O = AF0 + 1024
        X2TM_O = XTM_O + 1024
        WSTG_O = X2TM_O + 1024
        KVST_O = WSTG_O + 1024
        T1_O = KVST_O + 520
        assert T1_O + 1024 <= 6296
        X2TMB_O = AF0
        T3O_O = AF0 + 1024
        T3I_O = T3O_O + 1024
        OUTS_O = T3I_O + 1024
        UBC_O = OUTS_O + 1024
        assert UBC_O + 1024 <= 6296

        def b(po, off, n):
            return AB[po[0]:po[1], off:off + n]

        def f(po, off, n):
            return AFR[po[0]:po[1], off:off + n]

        FULL = (0, 128)

        Bu = lambda: Buf()
        bufs = {}
        def B_(name):
            if name not in bufs:
                bufs[name] = Buf()
            return bufs[name]
        pq0, pq1, pw0, pw1, pc, pd, pe, p7 = (Buf() for _ in range(8))

        def dma(engine, out_ap, in_ap, reads=(), writes=(), deps=()):
            return P.add_dma(engine, lambda e, o=out_ap, i=in_ap: e.dma_start(out=o, in_=i),
                             reads=reads, writes=writes, deps=deps)

        iotaj = B_("iotaj"); iotap = B_("iotap")
        P.add("gpsimd", lambda e: e.iota(f(FULL, XTM_O, 128), pattern=[[1, 128]], base=0,
                                         channel_multiplier=0, allow_small_or_imprecise_dtypes=True),
              writes=[iotaj])
        P.add("gpsimd", lambda e: e.iota(f(FULL, X2TM_O, 128), pattern=[[0, 128]], base=0,
                                         channel_multiplier=1, allow_small_or_imprecise_dtypes=True),
              writes=[iotap])
        identb = B_("identb"); identf = B_("identf"); maskf = B_("maskf")
        P.add("vector", lambda e: e.tensor_tensor(out=b(FULL, IDENTB_O, 128), in0=f(FULL, XTM_O, 128),
                                                  in1=f(FULL, X2TM_O, 128), op=OP.is_equal),
              reads=[iotaj, iotap], writes=[identb])
        P.add("vector", lambda e: e.tensor_copy(out=f(FULL, IDENTF_O, 128), in_=b(FULL, IDENTB_O, 128)),
              reads=[identb], writes=[identf])
        P.add("vector", lambda e: e.tensor_tensor(out=f(FULL, MASKF_O, 128), in0=f(FULL, X2TM_O, 128),
                                                  in1=f(FULL, XTM_O, 128), op=OP.is_le),
              reads=[iotaj, iotap], writes=[maskf])
        mask4 = B_("mask4")
        for q4 in range(4):
            P.add("vector", lambda e, q4=q4: e.tensor_copy(out=b(FULL, MASK4B_O + q4 * 128, 128),
                                                           in_=f(FULL, MASKF_O, 128)),
                  reads=[maskf], writes=[mask4])
        ones = B_("ones")
        P.add("vector", lambda e: e.memset(b(FULL, ONECOL_O, 1), 1.0), writes=[ones])
        P.add("vector", lambda e: e.memset(AB[0:1, ONEROW_O:ONEROW_O + 128], 1.0), writes=[ones])
        P.add("vector", lambda e: e.memset(AB[32:33, ONEROW_O:ONEROW_O + 128], 1.0), writes=[ones])
        P.add("vector", lambda e: e.memset(AB[64:65, ONEROW_O:ONEROW_O + 128], 1.0), writes=[ones])
        P.add("vector", lambda e: e.memset(f((0, 1), ONEROWF_O, 128), 1.0), writes=[ones])
        P.add("vector", lambda e: e.memset(f(FULL, ONECF_O, 1), 1.0), writes=[ones])
        consts = B_("consts")
        NWT = float(D * FF)
        for idx, val in enumerate([LN_EPS, -0.5, 0.5 / NWT, 0.5 * ABS_EPS, -1.0, MIX_EPS]):
            P.add("vector", lambda e, idx=idx, val=val: e.memset(f(FULL, CONSTS_O + idx, 1), val),
                  writes=[consts])
        C_EPS, C_NEGH, C_HALFNW, C_HALFEPS, C_NEGONE, C_MEPS = (CONSTS_O + i for i in range(6))
        kv0 = B_("kv0"); kv1 = B_("kv1"); kvb0 = B_("kvb0"); kvb1 = B_("kvb1")
        P.add("vector", lambda e: e.memset(f(FULL, KVST_O, 520), 0.0), writes=[kv0])
        P.add("vector", lambda e: e.memset(b(FULL, KVB_O, 520), 0.0), writes=[kvb0])

        bqk = B_("bqk"); rows = B_("rows"); lnwb = B_("lnwb")
        dma("sync", f(FULL, BQK_O, 16), bqk_d, writes=[bqk])
        P.add("vector", lambda e: e.tensor_scalar(out=f(FULL, BQK1_O, 16), in0=f(FULL, BQK_O, 16),
                                                  scalar1=1.0, scalar2=None, op0=OP.add),
              reads=[bqk], writes=[bqk])
        t1b = B_("t1b")
        dma("sync", f((0, 1), T1_O, 1024), bv_d, writes=[t1b])
        P.add("vector", lambda e: e.tensor_copy(out=AB[0:1, BVROW_O:BVROW_O + 1024],
                                                in_=f((0, 1), T1_O, 1024)),
              reads=[t1b], writes=[rows])
        dma("sync", AFR[32:33, T1_O:T1_O + 1024], bproj_d, reads=[rows], writes=[t1b])
        P.add("vector", lambda e: e.tensor_copy(out=AB[32:33, BPROW_O:BPROW_O + 1024],
                                                in_=AFR[32:33, T1_O:T1_O + 1024]),
              reads=[t1b], writes=[rows])
        dma("sync", f(FULL, LNW_O, 32), lnw_d, writes=[lnwb])
        dma("sync", f(FULL, LNB_O, 32), lnb_d, writes=[lnwb])
        wbbf = B_("wbbf")
        P.add("vector", lambda e: e.tensor_copy(out=AB[:, WBBF2_O:WBBF2_O + 64][:, 0:64:2],
                                                in_=f(FULL, LNW_O, 32)),
              reads=[lnwb], writes=[wbbf])
        P.add("vector", lambda e: e.tensor_copy(out=AB[:, WBBF2_O + 1:WBBF2_O + 64][:, 0:63:2],
                                                in_=f(FULL, LNB_O, 32)),
              reads=[lnwb], writes=[wbbf])

        wqk = [B_(f"wqk{k}") for k in range(KC)]
        wv = [B_(f"wv{k}") for k in range(KC)]
        wpj = [B_(f"wpj{k}") for k in range(KC)]
        wstg = [B_("wstg0"), B_("wstg1")]
        for k in range(KC):
            for j in range(4):
                s = wstg[(k * 8 + j) % 2]
                dma("sync", f(FULL, WSTG_O + ((k * 8 + j) % 2) * 512, 512),
                    wattn_d[k * 128:(k + 1) * 128, j * 512:(j + 1) * 512], writes=[s])
                P.add("vector", lambda e, k=k, j=j, m=(k * 8 + j) % 2:
                      e.tensor_copy(out=b(FULL, WQK_O + k * 2048 + j * 512, 512),
                                    in_=f(FULL, WSTG_O + m * 512, 512)),
                      reads=[s], writes=[wqk[k]])
            for j in range(2):
                s = wstg[(k * 8 + 4 + j) % 2]
                dma("sync", f(FULL, WSTG_O + ((k * 8 + 4 + j) % 2) * 512, 512),
                    wattn_d[k * 128:(k + 1) * 128, 2048 + j * 512:2048 + (j + 1) * 512], writes=[s])
                P.add("vector", lambda e, k=k, j=j, m=(k * 8 + 4 + j) % 2:
                      e.tensor_copy(out=b(FULL, WV_O + k * 1024 + j * 512, 512),
                                    in_=f(FULL, WSTG_O + m * 512, 512)),
                      reads=[s], writes=[wv[k]])
            for j in range(2):
                s = wstg[(k * 8 + 6 + j) % 2]
                dma("sync", f(FULL, WSTG_O + ((k * 8 + 6 + j) % 2) * 512, 512),
                    wproj_d[k * 128:(k + 1) * 128, j * 512:(j + 1) * 512], writes=[s])
                P.add("vector", lambda e, k=k, j=j, m=(k * 8 + 6 + j) % 2:
                      e.tensor_copy(out=b(FULL, WPROJ_O + k * 1024 + j * 512, 512),
                                    in_=f(FULL, WSTG_O + m * 512, 512)),
                      reads=[s], writes=[wpj[k]])

        xfm = [[B_(f"xfm{bi}_{k}") for k in range(KC)] for bi in range(2)]
        xstg = [B_("xstg0"), B_("xstg1")]

        def load_xfm(sc, dep_ops=()):
            seq, t0 = sc // 4, (sc % 4) * SCT
            bi = sc % 2
            for k in range(KC):
                s = xstg[k % 2]
                so = XFMSTG_O if k % 2 == 0 else XFMSTG2_O
                dma("sync", f(FULL, so, 512),
                    x_fm_d[seq * D + k * 128: seq * D + (k + 1) * 128, t0:t0 + SCT],
                    writes=[s], deps=dep_ops)
                P.add("vector", lambda e, k=k, so=so, bi=bi:
                      e.tensor_copy(out=b(FULL, XFM_O + bi * 4096 + k * 512, 512),
                                    in_=f(FULL, so, 512)),
                      reads=[s], writes=[xfm[bi][k]])

        load_xfm(0)

        qfm = [B_(f"qfm{m}") for m in range(8)]
        kfm = [B_(f"kfm{m}") for m in range(8)]
        estg = [B_("estg0"), B_("estg1")]

        def qk_mms(sc):
            bi = sc % 2
            for m in range(16):
                pq, pqb = (ps_qk0, pq0) if m % 2 == 0 else (ps_qk1, pq1)
                def mmfn(e, m=m, pq=pq, bi=bi):
                    for k in range(KC):
                        ins = e.matmul(pq[:, 0:512], b(FULL, WQK_O + k * 2048 + m * 128, 128),
                                       b(FULL, XFM_O + bi * 4096 + k * 512, 512),
                                       start=(k == 0), stop=(k == KC - 1))
                    return ins
                hmm = P.add("tensor", mmfn,
                            reads=[wqk[k] for k in range(KC)] + [xfm[bi][k] for k in range(KC)],
                            writes=[pqb])
                dest_o = (QFM_O + m * 512) if m < 8 else (KFM_O + (m - 8) * 512)
                dest_b = qfm[m] if m < 8 else kfm[m - 8]
                es = estg[m % 2]
                P.add("scalar", lambda e, m=m, pq=pq, eo=ESTG_O + (m % 2) * 512:
                      e.activation(out=b(FULL, eo, 512), in_=pq[:, 0:512], func=AF.Exp,
                                   bias=f(FULL, BQK_O + m, 1), scale=1.0),
                      reads=[pqb, bqk], writes=[es])
                P.add("scalar", lambda e, pq=pq, do=dest_o, m=m:
                      e.activation(out=b(FULL, do, 512), in_=pq[:, 0:512], func=AF.Relu,
                                   bias=f(FULL, BQK_O + m, 1), scale=1.0),
                      reads=[pqb, bqk], writes=[dest_b])
                P.add("vector", lambda e, do=dest_o, eo=ESTG_O + (m % 2) * 512:
                      e.scalar_tensor_tensor(out=b(FULL, do, 512), in0=b(FULL, do, 512),
                                             scalar=1.0, in1=b(FULL, eo, 512),
                                             op0=OP.add, op1=OP.min),
                      reads=[es, dest_b], writes=[dest_b])

        vtm = B_("vtm")
        ktm = B_("ktm")
        at = [B_("ath0"), B_("ath1"), B_("ath0"), B_("ath1")]
        ytm = B_("ytm")
        yfm = B_("yfm")
        x2tm = B_("x2tm")
        x2fmb = B_("x2fmb")
        xtm = B_("xtm")
        t1 = B_("t1")
        stats = B_("stats")
        recs = B_("recs")

        def chunk_ln1(cg):
            if ON('ln1'):
                P.add("vector", lambda e: e.tensor_tensor(out=f(FULL, T1_O, 1024), in0=ps_wide[:, 0:1024],
                                                          in1=f(FULL, XTM_O, 1024), op=OP.add),
                      reads=[pw0, pw1, xtm], writes=[t1])
                P.add("vector", lambda e: e.bn_stats(out=f(FULL, BNS_O, 6), in_=f(FULL, T1_O, 512)),
                      reads=[t1], writes=[stats])
                P.add("vector", lambda e: e.bn_stats(out=f(FULL, BNS_O + 6, 6), in_=f(FULL, T1_O + 512, 512)),
                      reads=[t1], writes=[stats])
                P.add("vector", lambda e: e.bn_aggr(out=f(FULL, AGG_O, 2), in_=f(FULL, BNS_O, 12)),
                      reads=[stats], writes=[stats])
                P.add("scalar", lambda e: e.activation(out=f(FULL, R1_O, 1), in_=f(FULL, AGG_O + 1, 1),
                                                       func=AF.Ln, bias=f(FULL, C_EPS, 1), scale=1.0),
                      reads=[stats, consts], writes=[stats])
                P.add("scalar", lambda e: e.activation(out=f(FULL, R1_O + 1, 1), in_=f(FULL, R1_O, 1),
                                                       func=AF.Exp, bias=0.0, scale=f(FULL, C_NEGH, 1)),
                      reads=[stats, consts], writes=[stats])
                P.add("vector", lambda e: e.tensor_scalar(out=f(FULL, NM_O, 1), in0=f(FULL, AGG_O, 1),
                                                          scalar1=f(FULL, R1_O + 1, 1), scalar2=-1.0,
                                                          op0=OP.mult, op1=OP.mult),
                      reads=[stats], writes=[stats])
                P.add("scalar", lambda e: e.activation(out=f(FULL, X2TM_O, 1024), in_=f(FULL, T1_O, 1024),
                                                       func=AF.Identity, bias=f(FULL, NM_O, 1),
                                                       scale=f(FULL, R1_O + 1, 1)),
                      reads=[t1, stats], writes=[x2tm])
                dma("gpsimd", x2tm_s[cg * 128:(cg + 1) * 128, :], f(FULL, X2TM_O, 1024), reads=[x2tm])


        def chunk_x2t(cg):
            if not ON('x2t'):
                return
            for r in range(2):
                def x2t(e, r=r):
                    for j in range(4):
                        fc_ = 4 * r + j
                        ins = e.transpose(ps_7[:, j * 128:(j + 1) * 128],
                                          f(FULL, X2TM_O + fc_ * 128, 128), f(FULL, IDENTF_O, 128))
                    return ins
                P.add("tensor", x2t, reads=[x2tm, identf], writes=[p7])
                P.add("scalar", lambda e, r=r:
                      e.copy(out=b(FULL, X2FMB_O + r * 512, 512), in_=ps_7[:, 0:512]),
                      reads=[p7], writes=[x2fmb])
            seq, t0 = cg // 16, (cg % 16) * 128
            for j in range(KC):
                dma("gpsimd", x2fm_s[seq * D + j * 128: seq * D + (j + 1) * 128, t0:t0 + 128],
                    AB[:, X2FMB_O + j * 128:X2FMB_O + (j + 1) * 128], reads=[x2fmb])

        def chunk_attn(cg, filler=None, pre=None):
            sc, c = cg // CPS, cg % CPS
            seq = sc // 4
            heads_round = [[0, 2, 4, 6], [1, 3, 5, 7], [8, 10, 12, 14], [9, 11, 13, 15]]
            kvr = kvw = kv0
            kvbr = kvbw = kvb0
            kvr_o = kvw_o = KVST_O
            kvbr_o = kvbw_o = KVB_O
            seq_last = (cg % 16) == 15

            if ON('ktr'):
                def ktr(e):
                    for fch in range(KC):
                        ins = e.transpose(ps_e[:, fch * 128:(fch + 1) * 128],
                                          AB[:, KFM_O + fch * 512 + c * 128:
                                             KFM_O + fch * 512 + (c + 1) * 128],
                                          b(FULL, IDENTB_O, 128))
                    return ins
                P.add("tensor", ktr, reads=[kfm[m] for m in range(8)] + [identb], writes=[pe])
                P.add("scalar", lambda e: e.copy(out=b(FULL, KTM_O, 1024), in_=ps_e[:, 0:1024]),
                      reads=[pe], writes=[ktm])

            mixbank = [(ps_c, pc), (ps_d, pd), (ps_wide[:, 0:512], pw0),
                       (ps_wide[:, 512:1024], pw1)]

            def emit_st(r):
                if not ON('st'):
                    return
                psr, pbr = mixbank[r]
                def stfn(e, r=r, psr=psr, c=c):
                    for j in range(4):
                        h = heads_round[r][j]
                        fch, half = h // 2, h % 2
                        ksl = AB[64 * half:64 * half + 64,
                                 KFM_O + fch * 512 + c * 128: KFM_O + fch * 512 + (c + 1) * 128]
                        qsl = AB[64 * half:64 * half + 64,
                                 QFM_O + fch * 512 + c * 128: QFM_O + fch * 512 + (c + 1) * 128]
                        ins = e.matmul(psr[:, j * 128:(j + 1) * 128], ksl, qsl, start=True, stop=True)
                    return ins
                P.add("tensor", stfn, reads=[qfm[m] for m in range(8)] + [kfm[m] for m in range(8)],
                      writes=[pbr])
                P.add("vector", lambda e, r=r, psr=psr:
                      e.tensor_tensor(out=b(FULL, AT_O + (r % 2) * 512, 512), in0=psr[:, 0:512],
                                  in1=b(FULL, MASK4B_O, 512), op=OP.mult),
                      reads=[pbr, mask4], writes=[at[r]])

            def emit_num(r):
                if not ON('num'):
                    return
                psr, pbr = mixbank[r]
                def numfn(e, r=r, psr=psr, c=c, kvbr_o=kvbr_o):
                    for j in range(4):
                        h = heads_round[r][j]
                        fch, half = h // 2, h % 2
                        atsl = b(FULL, AT_O + (r % 2) * 512 + j * 128, 128)
                        qsl = AB[64 * half:64 * half + 64,
                                 QFM_O + fch * 512 + c * 128: QFM_O + fch * 512 + (c + 1) * 128]
                        kvo = kvbr_o + (h // 2) * 65
                        kv_num = AB[64 * half:64 * half + 64, kvo:kvo + 64]
                        kv_ks = AB[64 * half:64 * half + 64, kvo + 64:kvo + 65]
                        e.matmul(psr[:, j * 65:j * 65 + 64], atsl, b(FULL, VTM_O + h * 64, 64),
                                 start=(j == 0), stop=False)
                        e.matmul(psr[:, j * 65 + 64:j * 65 + 65], atsl, b(FULL, ONECOL_O, 1),
                                 start=False, stop=False)
                        e.matmul(psr[:, j * 65:j * 65 + 64], qsl, kv_num, start=False, stop=False)
                        ins = e.matmul(psr[:, j * 65 + 64:j * 65 + 65], qsl, kv_ks,
                                   start=False, stop=(j == 3))
                    return ins
                P.add("tensor", numfn, reads=[at[r], vtm, ones, kvbr] + [qfm[m] for m in range(8)],
                      writes=[pbr])
                P.add("scalar", lambda e, r=r, psr=psr:
                      e.activation(out=f(FULL, DSTG_O, 4), in_=psr[:, 64:64 + 3 * 65 + 1:65],
                                   func=AF.Ln, bias=f(FULL, C_MEPS, 1), scale=1.0),
                      reads=[pbr, at[r], consts], writes=[stats])
                P.add("scalar", lambda e, r=r: e.activation(out=f(FULL, RECS_O + 4 * r, 4),
                                                            in_=f(FULL, DSTG_O, 4), func=AF.Exp,
                                                            bias=0.0, scale=f(FULL, C_NEGONE, 1)),
                      reads=[stats, consts], writes=[recs])
                def ysc(e, r=r, psr=psr):
                    for j in range(4):
                        h = heads_round[r][j]
                        ins = e.activation(out=b(FULL, YTM_O + h * 64, 64),
                                       in_=psr[:, j * 65:j * 65 + 64], func=AF.Copy,
                                       bias=0.0, scale=f(FULL, RECS_O + 4 * r + j, 1))
                    return ins
                P.add("scalar", ysc, reads=[pbr, recs], writes=[ytm])

            if ON('st'):
                emit_st(0)
                emit_st(1)
            if ON('v'):
                for half in range(2):
                    psh = ps_qk0[:, 0:512] if half == 0 else ps_qk1[:, 0:512]
                    pb = pq0 if half == 0 else pq1
                    bi = sc % 2
                    def vfn(e, half=half, psh=psh, c=c, bi=bi):
                        for k in range(KC):
                            xo = XFM_O + bi * 4096 + k * 512 + c * 128
                            e.matmul(psh, AB[:, xo:xo + 128],
                                     b(FULL, WV_O + k * 1024 + half * 512, 512),
                                     start=(k == 0), stop=False)
                        return e.matmul(psh, b((0, 1), ONEROW_O, 128),
                                        b((0, 1), BVROW_O + half * 512, 512), start=False, stop=True)
                    P.add("tensor", vfn, reads=[xfm[sc % 2][k] for k in range(KC)]
                          + [wv[k] for k in range(KC)] + [rows, ones], writes=[pb])
                    P.add("vector", lambda e, half=half, psh=psh:
                          e.tensor_copy(out=b(FULL, VTM_O + half * 512, 512), in_=psh),
                          reads=[pb], writes=[vtm])
            if cg > 0 and ON('ln1'):
                chunk_ln1(cg - 1)
            if ON('proj'):
                dma("sync", f(FULL, XTM_O, 1024), x_tm_d[cg * 128:(cg + 1) * 128, :], writes=[xtm])

            if ON('st'):
                emit_num(0)
                emit_st(2)
                emit_num(1)
                emit_st(3)
                emit_num(2)
                emit_num(3)

            if ON('kv'):
                if not seq_last:
                    for r in range(4):
                        psr, pbr = mixbank[(r + 1) % 4]
                        def kvfn(e, r=r, psr=psr):
                            for half in range(2):
                                bp = 64 * half
                                heads = [4 * r + half, 4 * r + 2 + half]
                                for idx, h in enumerate(heads):
                                    ktsl = b(FULL, KTM_O + h * 64, 64)
                                    col = (h // 2) % 2
                                    e.matmul(psr[bp:bp + 64, col * 65:col * 65 + 64],
                                             ktsl, b(FULL, VTM_O + h * 64, 64),
                                             start=(idx == 0), stop=False)
                                    ins = e.matmul(psr[bp:bp + 64, col * 65 + 64:col * 65 + 65],
                                                   ktsl, b(FULL, ONECOL_O, 1), start=False,
                                                   stop=(idx == 1))
                            return ins
                        P.add("tensor", kvfn, reads=[ktm, vtm, ones], writes=[pbr])
                        P.add("vector", lambda e, r=r, psr=psr, kvr_o=kvr_o, kvw_o=kvw_o:
                              e.scalar_tensor_tensor(out=f(FULL, kvw_o + r * 130, 130),
                                                     in0=psr[:, 0:130], scalar=1.0,
                                                     in1=f(FULL, kvr_o + r * 130, 130),
                                                     op0=OP.mult, op1=OP.add),
                              reads=[pbr, kvr], writes=[kvw])
                    if r == 3:
                        P.add("scalar", lambda e, kvw_o=kvw_o, kvbw_o=kvbw_o:
                              e.copy(out=b(FULL, kvbw_o, 520), in_=f(FULL, kvw_o, 520)),
                              reads=[kvw], writes=[kvbw])
                else:
                    P.add("vector", lambda e, kvw_o=kvw_o: e.memset(f(FULL, kvw_o, 520), 0.0),
                          reads=[kvr], writes=[kvw])
                    P.add("vector", lambda e, kvbw_o=kvbw_o: e.memset(b(FULL, kvbw_o, 520), 0.0),
                          reads=[kvbr], writes=[kvbw])

            if pre is not None:
                pre()
            if filler is not None:
                filler()

            if ON('yt'):
                def ytr(e):
                    for fch in range(KC):
                        ins = e.transpose(ps_e[:, fch * 128:(fch + 1) * 128],
                                          b(FULL, YTM_O + fch * 128, 128),
                                          b(FULL, IDENTB_O, 128))
                    return ins
                P.add("tensor", ytr, reads=[ytm, identb], writes=[pe])
                P.add("scalar", lambda e: e.copy(out=b(FULL, YFM_O, 1024), in_=ps_e[:, 0:1024]),
                      reads=[pe], writes=[yfm])

            if ON('proj'):
                pass

                for half in range(2):
                    psh = ps_wide[:, half * 512:(half + 1) * 512]
                    pb = pw0 if half == 0 else pw1
                    def pjfn(e, half=half, psh=psh):
                        for k in range(KC):
                            e.matmul(psh, b(FULL, YFM_O + k * 128, 128),
                                     b(FULL, WPROJ_O + k * 1024 + half * 512, 512),
                                     start=(k == 0), stop=False)
                        return e.matmul(psh, AB[32:33, ONEROW_O:ONEROW_O + 128],
                                        AB[32:33, BPROW_O + half * 512:BPROW_O + half * 512 + 512],
                                        start=False, stop=True)
                    P.add("tensor", pjfn, reads=[yfm, rows, ones] + [wpj[k] for k in range(KC)],
                          writes=[pb])

        gacc = B_("gacc"); thrp = B_("thrp"); gs = B_("gs"); throw_b = B_("throw")
        bstg = B_("bstg"); wqstg = B_("wqstg")
        wqfc = [B_(f"wqfc{k}") for k in range(KC)]
        NW = D * FF

        def gamma_slice(widx, i):
            src = wfct_d if widx == 0 else wpft_d
            sl = src[(i // 8) * 128:(i // 8) * 128 + 128, (i % 8) * 512:(i % 8) * 512 + 512] \
                if widx == 0 else src[(i // 2) * 128:(i // 2) * 128 + 128,
                                      (i % 2) * 512:(i % 2) * 512 + 512]
            sb = wstg[i % 2]
            dma("sync", f(FULL, WSTG_O + (i % 2) * 512, 512), sl, writes=[sb])
            es = estg[i % 2]
            P.add("scalar", lambda e, i=i: e.activation(out=b(FULL, ESTG_O + (i % 2) * 512, 512),
                                                        in_=f(FULL, WSTG_O + (i % 2) * 512, 512),
                                                        func=AF.Abs,
                                                        accum_out=f(FULL, GACC_O + i, 1)),
                  reads=[sb], writes=[es, gacc])

        def gamma_fin(widx):
            P.add("vector", lambda e: e.tensor_reduce(out=f(FULL, GSUM_O, 1), in_=f(FULL, GACC_O, 64),
                                                      axis=mybir.AxisListType.X, op=OP.add),
                  reads=[gacc], writes=[gs])
            P.add("tensor", lambda e: e.matmul(ps_7[0:1, 0:1], f(FULL, GSUM_O, 1),
                                               f(FULL, ONECF_O, 1), start=True, stop=True),
                  reads=[gs, ones], writes=[p7])
            P.add("vector", lambda e: e.tensor_copy(out=f((0, 1), ONEROWF_O + 129, 1),
                                                    in_=ps_7[0:1, 0:1]),
                  reads=[p7], writes=[gs])
            P.add("scalar", lambda e: e.activation(out=f((0, 1), ONEROWF_O + 128, 1),
                                                   in_=f((0, 1), ONEROWF_O + 129, 1),
                                                   func=AF.Identity,
                                                   bias=f((0, 1), C_HALFEPS, 1),
                                                   scale=f((0, 1), C_HALFNW, 1)),
                  reads=[gs, consts], writes=[throw_b])
            P.add("scalar", lambda e: e.activation(out=f((0, 1), ONEROWF_O + 129, 1),
                                                   in_=f((0, 1), ONEROWF_O + 128, 1),
                                                   func=AF.Identity, bias=0.0,
                                                   scale=f((0, 1), C_NEGONE, 1)),
                  reads=[throw_b, consts], writes=[throw_b])
            P.add("tensor", lambda e: e.matmul(ps_7[:, 0:2], f((0, 1), ONEROWF_O, 128),
                                               f((0, 1), ONEROWF_O + 128, 2), start=True, stop=True),
                  reads=[throw_b, ones], writes=[p7])
            P.add("vector", lambda e, widx=widx: e.tensor_copy(out=f(FULL, THRP_O + 2 * widx, 2),
                                                               in_=ps_7[:, 0:2]),
                  reads=[p7], writes=[thrp])

        def quant_slice(widx, i):
            src = wfct_d if widx == 0 else wpft_d
            if widx == 0:
                sl = src[(i // 8) * 128:(i // 8) * 128 + 128, (i % 8) * 512:(i % 8) * 512 + 512]
            else:
                sl = src[(i // 2) * 128:(i // 2) * 128 + 128, (i % 2) * 512:(i % 2) * 512 + 512]
            sb = wstg[i % 2]
            dma("sync", f(FULL, WSTG_O + (i % 2) * 512, 512), sl, writes=[sb])
            P.add("vector", lambda e, i=i, widx=widx:
                  e.tensor_scalar(out=b(FULL, BSTG_O, 512), in0=f(FULL, WSTG_O + (i % 2) * 512, 512),
                                  scalar1=f(FULL, THRP_O + 2 * widx + 1, 1), scalar2=None,
                                  op0=OP.is_le),
                  reads=[sb, thrp], writes=[bstg])
            if widx == 0:
                dst_o = WQFC_O + (i // 8) * 4096 + (i % 8) * 512
                dst_b = wqfc[i // 8]
            else:
                dst_o = WQSTG_O
                dst_b = wqstg
            P.add("vector", lambda e, i=i, widx=widx, dst_o=dst_o:
                  e.scalar_tensor_tensor(out=b(FULL, dst_o, 512),
                                         in0=f(FULL, WSTG_O + (i % 2) * 512, 512),
                                         scalar=f(FULL, THRP_O + 2 * widx, 1),
                                         in1=b(FULL, BSTG_O, 512), op0=OP.is_ge, op1=OP.subtract),
                  reads=[sb, bstg, thrp], writes=[dst_b])
            if widx == 1:
                hc_, half = i // 2, i % 2
                dma("gpsimd", wqp_s[hc_ * 128:(hc_ + 1) * 128, half * 512:(half + 1) * 512],
                    b(FULL, WQSTG_O, 512), reads=[wqstg])

        def a_tasks_for_chunk(cg):
            sc, c = cg // 4, cg % 4
            i0 = (cg % 8) * 8
            if sc in (0, 1):
                for i in range(i0, i0 + 8):
                    gamma_slice(0, i)
                if cg == 7:
                    gamma_fin(0)
            elif sc in (2, 3):
                for i in range(i0, i0 + 8):
                    gamma_slice(1, i)
                if cg == 15:
                    gamma_fin(1)
            elif sc in (4, 5):
                for i in range(i0, i0 + 8):
                    quant_slice(0, i)
            else:
                j0 = (cg % 4) * 8 + (0 if sc == 6 else 1) * 32
                idxs = [2 * jj for jj in range(j0 % 32, j0 % 32 + 8)] if sc == 6 \
                    else [2 * jj + 1 for jj in range(j0 % 32, j0 % 32 + 8)]
                for i in idxs:
                    quant_slice(1, i)

        for sc in range(n_sc):
            if sc == 0 and ON('qk'):
                qk_mms(0)
            if sc + 1 < n_sc:
                load_xfm(sc + 1)
            for c in range(CPS):
                cg = sc * CPS + c
                pre = (lambda g=cg - 1: chunk_x2t(g)) if cg > 0 else None
                chunk_attn(cg, pre=pre)
                if ON('tasks'):
                    a_tasks_for_chunk(cg)
            if sc + 1 < n_sc and ON('qk'):
                qk_mms(sc + 1)
            if sc == n_sc - 1:
                if ON('ln1'):
                    chunk_ln1(n_sc * CPS - 1)
                chunk_x2t(n_sc * CPS - 1)

        ucrows = B_("ucrows")
        ucstg = B_("ucstg")

        if not phase_b:
            zb = B_("zerob")
            P.add("vector", lambda e: e.memset(f(FULL, T1_O, 1024), 0.0), writes=[zb])
            for cgz in range(TOK // 128):
                dma("gpsimd", out_d[cgz * 128:(cgz + 1) * 128, :], f(FULL, T1_O, 1024),
                    reads=[zb])
            P.emit(block, sems)
            return nc
        barrier = P.barrier_handles()

        wqpt = [Buf(init=barrier) for _ in range(HC)]
        wqps = Buf(init=barrier)
        wqps2 = Buf(init=barrier)
        hfm = [Buf(init=barrier) for _ in range(HC)]
        x2fmB = [Buf(init=barrier) for _ in range(KC)]
        x2tmB = Buf(init=barrier)
        t3o = Buf(init=barrier)
        t3i = Buf(init=barrier)
        outs_b = Buf(init=barrier)
        ubc = Buf(init=barrier)
        vb = Buf(init=barrier)
        rb = Buf(init=barrier)
        statB = Buf(init=barrier)

        def wqpt_dma(hc_):
            dma("sync", b(FULL, WQPSTG2_O if hc_ % 2 else WQPSTG_O, 1024),
                wqp_s[hc_ * 128:(hc_ + 1) * 128, :],
                writes=[wqps2 if hc_ % 2 else wqps])

        def wqpt_work(hc_):
            so = WQPSTG2_O if hc_ % 2 else WQPSTG_O
            sb_ = wqps2 if hc_ % 2 else wqps
            for half in range(2):
                P.add("tensor", lambda e, hc_=hc_, half=half, so=so:
                      e.matmul(ps_7[64 * half:64 * half + 2, 0:512],
                               AB[:, WBBF2_O + 2 * hc_:WBBF2_O + 2 * hc_ + 2],
                               b(FULL, so + half * 512, 512),
                               start=(hc_ == 0), stop=(hc_ == HC - 1)),
                      reads=[sb_, wbbf], writes=[p7])
            P.add("vector", lambda e, hc_=hc_, so=so:
                  e.tensor_scalar(out=b(FULL, WQPT_O + hc_ * 1024, 1024), in0=b(FULL, so, 1024),
                                  scalar1=f(FULL, LNW_O + hc_, 1), scalar2=None, op0=OP.mult),
                  reads=[sb_, lnwb], writes=[wqpt[hc_]])

        def uc_finish():
            P.add("vector", lambda e: e.tensor_copy(out=AB[0:2, WQPSTG_O:WQPSTG_O + 512],
                                                    in_=ps_7[0:2, 0:512]),
                  reads=[p7], writes=[ucstg, wqps])
            P.add("vector", lambda e: e.tensor_copy(out=AB[64:66, WQPSTG_O + 512:WQPSTG_O + 1024],
                                                    in_=ps_7[64:66, 0:512]),
                  reads=[p7], writes=[ucstg, wqps])
            dma("sync", AB[64:65, UROW_O:UROW_O + 512], AB[0:1, WQPSTG_O:WQPSTG_O + 512],
                reads=[ucstg], writes=[ucrows])
            dma("sync", AB[0:1, CROW_O:CROW_O + 512], AB[1:2, WQPSTG_O:WQPSTG_O + 512],
                reads=[ucstg], writes=[ucrows])
            dma("sync", AB[64:65, UROW_O + 512:UROW_O + 1024],
                AB[64:65, WQPSTG_O + 512:WQPSTG_O + 1024], reads=[ucstg], writes=[ucrows])
            dma("sync", AB[0:1, CROW_O + 512:CROW_O + 1024],
                AB[65:66, WQPSTG_O + 512:WQPSTG_O + 1024], reads=[ucstg], writes=[ucrows])
            for half in range(2):
                P.add("tensor", lambda e, half=half:
                      e.matmul(ps_7[:, 0:512], AB[64:65, ONEROW_O:ONEROW_O + 128],
                               AB[64:65, UROW_O + half * 512:UROW_O + half * 512 + 512],
                               start=True, stop=True),
                      reads=[ucrows, ones], writes=[p7])
                P.add("vector", lambda e, half=half:
                      e.tensor_copy(out=f(FULL, UBC_O + half * 512, 512), in_=ps_7[:, 0:512]),
                      reads=[p7], writes=[ubc])

        fcps = [(ps_qk0[:, 0:512], pq0), (ps_qk1[:, 0:512], pq1), (ps_wide[:, 0:512], pw0), (ps_wide[:, 512:1024], pw1)]

        def load_x2fm(sc):
            seq, t0 = sc // 4, (sc % 4) * SCT
            for k in range(KC):
                dma("sync", b(FULL, X2FMBB_O + k * 512, 512),
                    x2fm_s[seq * D + k * 128: seq * D + (k + 1) * 128, t0:t0 + SCT],
                    writes=[x2fmB[k]])

        load_x2fm(0)

        t3s_bufs = [Buf(init=barrier) for _ in range(TOK // 128)]

        def apply_ln3(cg, rcol):
            dma("sync", f(FULL, T3I_O, 1024), t3_s[cg * 128:(cg + 1) * 128, :],
                reads=[t3s_bufs[cg]], writes=[t3i])
            P.add("vector", lambda e, rcol=rcol:
                  e.tensor_scalar(out=f(FULL, NM_O + 1, 1), in0=f(FULL, MU3_O + rcol - 4, 1),
                                  scalar1=f(FULL, RB_O + rcol, 1), scalar2=-1.0,
                                  op0=OP.mult, op1=OP.mult),
              reads=[rb, statB], writes=[statB])
            P.add("scalar", lambda e, rcol=rcol:
                  e.activation(out=f(FULL, OUTS_O, 1024), in_=f(FULL, T3I_O, 1024),
                               func=AF.Identity,
                               bias=f(FULL, NM_O + 1, 1), scale=f(FULL, RB_O + rcol, 1)),
                  reads=[t3i, rb, statB], writes=[outs_b])
            dma("gpsimd", out_d[cg * 128:(cg + 1) * 128, :], f(FULL, OUTS_O, 1024), reads=[outs_b])

        MU3_O = SS_O

        prev_mu3 = [None] * 4

        for sc in range(n_sc):
            for hc_ in range(HC):
                if sc == 0:
                    wqpt_dma(hc_)
                    if hc_ >= 1:
                        wqpt_work(hc_ - 1)
                psh, pb = fcps[hc_ % 4]
                def fcfn(e, hc_=hc_, psh=psh):
                    for k in range(KC):
                        ins = e.matmul(psh, AB[:, WQFC_O + k * 4096 + hc_ * 128:
                                               WQFC_O + k * 4096 + (hc_ + 1) * 128],
                                       b(FULL, X2FMBB_O + k * 512, 512),
                                       start=(k == 0), stop=(k == KC - 1))
                    return ins
                P.add("tensor", fcfn, reads=[wqfc[k] for k in range(KC)] + [x2fmB[k] for k in range(KC)],
                      writes=[pb])
                P.add("scalar", lambda e, hc_=hc_, psh=psh:
                      e.activation(out=b(FULL, HFM_O + hc_ * 512, 512), in_=psh, func=AF.Gelu),
                      reads=[pb], writes=[hfm[hc_]])

            if sc == 0:
                wqpt_work(HC - 1)
                uc_finish()
            for c in range(CPS):
                def ssfn(e, c=c):
                    for hc_ in range(HC):
                        sl = AB[:, HFM_O + hc_ * 512 + c * 128: HFM_O + hc_ * 512 + (c + 1) * 128]
                        ins = e.matmul(ps_7[:, 0:128], sl, sl, start=(hc_ == 0), stop=False)
                    for hc_ in range(HC):
                        sl = AB[:, HFM_O + hc_ * 512 + c * 128: HFM_O + hc_ * 512 + (c + 1) * 128]
                        ins = e.matmul(ps_7[:, 128:129], sl, b(FULL, ONECOL_O, 1),
                                       start=False, stop=(hc_ == HC - 1))
                    return ins
                P.add("tensor", ssfn, reads=[hfm[i] for i in range(HC)] + [ones], writes=[p7])
                P.add("vector", lambda e: e.tensor_tensor(out=f(FULL, T3O_O, 128),
                                                          in0=ps_7[:, 0:128],
                                                          in1=f(FULL, IDENTF_O, 128), op=OP.mult),
                      reads=[p7, identf], writes=[t3o])
                P.add("vector", lambda e, c=c: e.tensor_reduce(out=f(FULL, R1_O, 1),
                                                               in_=f(FULL, T3O_O, 128),
                                                               axis=mybir.AxisListType.X, op=OP.add),
                      reads=[t3o], writes=[statB])
                P.add("vector", lambda e, c=c: e.tensor_scalar(out=f(FULL, MU2_O + c, 1),
                                                               in0=ps_7[:, 128:129],
                                                               scalar1=1.0 / FF, scalar2=None,
                                                               op0=OP.mult),
                      reads=[p7], writes=[statB])
                P.add("vector", lambda e, c=c:
                      e.tensor_scalar(out=f(FULL, NM_O + 2, 1), in0=f(FULL, MU2_O + c, 1),
                                      scalar1=f(FULL, MU2_O + c, 1), scalar2=None, op0=OP.mult),
                      reads=[statB], writes=[statB])
                P.add("vector", lambda e, c=c:
                      e.scalar_tensor_tensor(out=f(FULL, VB_O + c, 1), in0=f(FULL, R1_O, 1),
                                             scalar=1.0 / FF, in1=f(FULL, NM_O + 2, 1),
                                             op0=OP.mult, op1=OP.subtract),
                      reads=[statB], writes=[vb])

            ncols = 8 if sc > 0 else 4
            P.add("scalar", lambda e, ncols=ncols: e.activation(out=f(FULL, RB_O, ncols),
                                                                in_=f(FULL, VB_O, ncols),
                                                                func=AF.Ln, bias=f(FULL, C_EPS, 1),
                                                                scale=1.0),
                  reads=[vb, consts], writes=[rb])
            P.add("scalar", lambda e, ncols=ncols: e.activation(out=f(FULL, RB_O, ncols),
                                                                in_=f(FULL, RB_O, ncols),
                                                                func=AF.Exp, bias=0.0,
                                                                scale=f(FULL, C_NEGH, 1)),
                  reads=[rb, consts], writes=[rb])

            if sc > 0:
                for c in range(CPS):
                    apply_ln3((sc - 1) * CPS + c, 4 + c)

            for c in range(CPS):
                cg = sc * CPS + c
                dma("sync", f(FULL, X2TMB_O, 1024), x2tm_s[cg * 128:(cg + 1) * 128, :],
                    writes=[x2tmB])
                for half in range(2):
                    psr, pbr = (ps_c, pc) if half == 0 else (ps_d, pd)
                    def gfn(e, c=c, half=half, psr=psr):
                        for hc_ in range(HC):
                            e.matmul(psr[:, 0:512],
                                     AB[:, HFM_O + hc_ * 512 + c * 128: HFM_O + hc_ * 512 + (c + 1) * 128],
                                     b(FULL, WQPT_O + hc_ * 1024 + half * 512, 512),
                                     start=(hc_ == 0), stop=False)
                        return e.matmul(psr[:, 0:512], b((0, 1), ONEROW_O, 128),
                                        b((0, 1), CROW_O + half * 512, 512), start=False, stop=True)
                    P.add("tensor", gfn, reads=[hfm[i] for i in range(HC)] + [wqpt[i] for i in range(HC)]
                          + [ucrows, ones], writes=[pbr])
                P.add("vector", lambda e, c=c:
                      e.tensor_scalar(out=f(FULL, NM2_O + c, 1), in0=f(FULL, MU2_O + c, 1),
                                      scalar1=f(FULL, RB_O + c, 1), scalar2=-1.0,
                                      op0=OP.mult, op1=OP.mult),
                      reads=[rb, statB], writes=[statB])
                P.add("vector", lambda e, c=c:
                      e.scalar_tensor_tensor(out=f(FULL, T3O_O, 1024), in0=f(FULL, UBC_O, 1024),
                                             scalar=f(FULL, NM2_O + c, 1), in1=f(FULL, X2TMB_O, 1024),
                                             op0=OP.mult, op1=OP.add),
                      reads=[ubc, x2tmB, statB], writes=[t3o])
                for half in range(2):
                    psr, pbr = (ps_c, pc) if half == 0 else (ps_d, pd)
                    P.add("vector", lambda e, c=c, half=half, psr=psr:
                          e.scalar_tensor_tensor(out=f(FULL, T3O_O + half * 512, 512),
                                                 in0=psr[:, 0:512], scalar=f(FULL, RB_O + c, 1),
                                                 in1=f(FULL, T3O_O + half * 512, 512),
                                                 op0=OP.mult, op1=OP.add),
                          reads=[pbr, rb, t3o], writes=[t3o])
                P.add("vector", lambda e: e.bn_stats(out=f(FULL, BNS_O, 6), in_=f(FULL, T3O_O, 512)),
                      reads=[t3o], writes=[statB])
                P.add("vector", lambda e: e.bn_stats(out=f(FULL, BNS_O + 6, 6),
                                                     in_=f(FULL, T3O_O + 512, 512)),
                      reads=[t3o], writes=[statB])
                P.add("vector", lambda e, c=c: e.bn_aggr(out=f(FULL, AGG_O, 2), in_=f(FULL, BNS_O, 12)),
                      reads=[statB], writes=[statB])
                P.add("vector", lambda e, c=c: e.tensor_copy(out=f(FULL, MU3_O + c, 1),
                                                             in_=f(FULL, AGG_O, 1)),
                      reads=[statB], writes=[statB])
                P.add("vector", lambda e, c=c: e.tensor_copy(out=f(FULL, VB_O + 4 + c, 1),
                                                             in_=f(FULL, AGG_O + 1, 1)),
                      reads=[statB], writes=[vb])
                dma("gpsimd", t3_s[cg * 128:(cg + 1) * 128, :], f(FULL, T3O_O, 1024),
                    reads=[t3o], writes=[t3s_bufs[cg]])
            if sc + 1 < n_sc:
                load_x2fm(sc + 1)

        P.add("scalar", lambda e: e.activation(out=f(FULL, RB_O + 4, 4), in_=f(FULL, VB_O + 4, 4),
                                               func=AF.Ln, bias=f(FULL, C_EPS, 1), scale=1.0),
              reads=[vb, consts], writes=[rb])
        P.add("scalar", lambda e: e.activation(out=f(FULL, RB_O + 4, 4), in_=f(FULL, RB_O + 4, 4),
                                               func=AF.Exp, bias=0.0, scale=f(FULL, C_NEGH, 1)),
              reads=[rb, consts], writes=[rb])
        for c in range(CPS):
            apply_ln3((n_sc - 1) * CPS + c, 4 + c)

        P.emit(block, sems)
    return nc


_NC_CACHE = {}


def _get_nc():
    if "nc" not in _NC_CACHE:
        _NC_CACHE["nc"] = build_core_program()
    return _NC_CACHE["nc"]


def kernel(L0_attn_x, L0_attn_W_attn, L0_attn_b_attn, L0_attn_W_proj, L0_attn_b_proj,
           L0_ffn_W_fc, L0_ffn_ln_w, L0_ffn_ln_b, L0_ffn_W_proj):
    x = np.ascontiguousarray(np.asarray(L0_attn_x, dtype=np.float32))
    wattn = np.ascontiguousarray(np.asarray(L0_attn_W_attn, dtype=np.float32))
    battn = np.asarray(L0_attn_b_attn, dtype=np.float32)
    wproj = np.ascontiguousarray(np.asarray(L0_attn_W_proj, dtype=np.float32))
    bproj = np.asarray(L0_attn_b_proj, dtype=np.float32)
    wfc = np.asarray(L0_ffn_W_fc, dtype=np.float32)
    lnw = np.asarray(L0_ffn_ln_w, dtype=np.float32)
    lnb = np.asarray(L0_ffn_ln_b, dtype=np.float32)
    wpf = np.asarray(L0_ffn_W_proj, dtype=np.float32)

    bqk = np.ascontiguousarray(battn[:2048].reshape(16, 128).T)
    bv = np.ascontiguousarray(battn[2048:].reshape(1, 1024))
    bproj_r = np.ascontiguousarray(bproj.reshape(1, 1024))
    wfct = np.ascontiguousarray(wfc.T)
    wpft = np.ascontiguousarray(wpf.T)
    lnw_l = np.ascontiguousarray(lnw.reshape(HC, 128).T)
    lnb_l = np.ascontiguousarray(lnb.reshape(HC, 128).T)

    in_maps = []
    for i in range(8):
        xc = x[2 * i:2 * i + 2]
        x_fm = np.ascontiguousarray(xc.transpose(0, 2, 1)).reshape(SEQS * D, T)
        x_tm = np.ascontiguousarray(xc.reshape(TOK, D))
        in_maps.append({
            "x_fm": x_fm, "x_tm": x_tm, "wattn": wattn, "bqk": bqk, "bv": bv,
            "bproj": bproj_r, "wproj": wproj, "wfct": wfct, "wpft": wpft,
            "lnw": lnw_l, "lnb": lnb_l,
        })

    nc = _get_nc()
    res = run_bass_kernel_spmd(nc, in_maps, core_ids=list(range(8)))
    out = np.concatenate([r["out"].reshape(2, T, D) for r in res.results], axis=0)
    return np.ascontiguousarray(out.astype(np.float32))


# version 17 (final)
# speedup vs baseline: 1.1522x; 1.1522x over previous
import numpy as np
import ml_dtypes
import concourse.bass as bass
import concourse.mybir as mybir
from concourse.alu_op_type import AluOpType
from concourse.bass_utils import run_bass_kernel_spmd

F32 = mybir.dt.float32
BF16 = mybir.dt.bfloat16
AF = mybir.ActivationFunctionType
OP = AluOpType

ENGINES = ("sync", "tensor", "vector", "scalar", "gpsimd")

D = 1024
H = 16
DH = 64
FF = 4096
T = 2048
SEQS = 2
TOK = SEQS * T
CH = 128
SCT = 512
NSC = TOK // SCT
CPS = SCT // CH
KC = D // 128
HC = FF // 128
LN_EPS = 1e-5
MIX_EPS = 1e-6
ABS_EPS = 1e-6


class Buf:
    __slots__ = ("w", "r")

    def __init__(self, init=()):
        self.w = list(init)
        self.r = []


NDMA = 8


class Prog:
    def __init__(self):
        self.semnames = list(ENGINES) + [f"sd{i}" for i in range(NDMA)] \
            + [f"gd{i}" for i in range(NDMA)]
        self.ops = {e: [] for e in ENGINES}
        self.count = {s: 0 for s in self.semnames}
        self.dma_idx = {"sync": 0, "gpsimd": 0}

    def add(self, eng, fn, reads=(), writes=(), deps=(), inc=1, sem=None):
        sem = sem or eng
        dep = list(deps)
        for b in reads:
            dep += b.w
        for b in writes:
            dep += b.r
            dep += b.w
        need = {}
        for (ds, dv) in dep:
            assert dv <= self.count[ds], "dependency on future op"
            need[ds] = max(need.get(ds, 0), dv)
        self.ops[eng].append((fn, inc, sem, need))
        self.count[sem] += inc
        h = (sem, self.count[sem])
        for b in reads:
            b.r.append(h)
        for b in writes:
            b.w = [h]
            b.r = []
        return h

    def add_dma(self, eng, fn, reads=(), writes=(), deps=()):
        i = self.dma_idx[eng] % NDMA
        self.dma_idx[eng] += 1
        sem = ("sd" if eng == "sync" else "gd") + str(i)
        deps = list(deps) + [(sem, self.count[sem])]
        return self.add(eng, fn, reads=reads, writes=writes, deps=deps, inc=16, sem=sem)

    def barrier_handles(self):
        return [(s, self.count[s]) for s in self.semnames]

    def emit(self, block, sems):
        for eng in ENGINES:
            def make(eng):
                def body(e):
                    water = {k: 0 for k in self.semnames}
                    for fn, inc, sem, need in self.ops[eng]:
                        for ds, dv in need.items():
                            if dv > water[ds]:
                                e.wait_ge(sems[ds], dv)
                                water[ds] = dv
                        ins = fn(e)
                        ins.then_inc(sems[sem], inc)
                        if sem == eng:
                            water[sem] = max(water[sem], self.count_at_emit(eng, sem))
                    if eng == "gpsimd":
                        for s in ["gpsimd"] + [f"gd{i}" for i in range(NDMA)]:
                            if self.count[s] > water[s]:
                                e.wait_ge(sems[s], self.count[s])
                return body
            getattr(block, eng)(make(eng))

    def count_at_emit(self, eng, sem):
        return 0


def build_core_program(debug=False, n_sc=NSC, phase_b=True, stages=None):
    if stages is None:
        stages = {"qk", "ktr", "v", "st", "num", "kv", "yt", "proj", "ln1", "x2t", "tasks"}
    ON = stages.__contains__
    nc = bass.Bass("TRN2", target_bir_lowering=False)

    x_fm_d = nc.dram_tensor("x_fm", [SEQS * D, T], F32, kind="ExternalInput").ap()
    x_tm_d = nc.dram_tensor("x_tm", [TOK, D], F32, kind="ExternalInput").ap()
    wattn_d = nc.dram_tensor("wattn", [D, 3 * D], F32, kind="ExternalInput").ap()
    bqk_d = nc.dram_tensor("bqk", [128, 16], F32, kind="ExternalInput").ap()
    bv_d = nc.dram_tensor("bv", [1, D], F32, kind="ExternalInput").ap()
    bproj_d = nc.dram_tensor("bproj", [1, D], F32, kind="ExternalInput").ap()
    wproj_d = nc.dram_tensor("wproj", [D, D], F32, kind="ExternalInput").ap()
    wfct_d = nc.dram_tensor("wfct", [D, FF], F32, kind="ExternalInput").ap()
    wpft_d = nc.dram_tensor("wpft", [FF, D], F32, kind="ExternalInput").ap()
    lnw_d = nc.dram_tensor("lnw", [128, HC], F32, kind="ExternalInput").ap()
    lnb_d = nc.dram_tensor("lnb", [128, HC], F32, kind="ExternalInput").ap()
    out_d = nc.dram_tensor("out", [TOK, D], F32, kind="ExternalOutput").ap()
    kind_dbg = "ExternalOutput" if debug else "Internal"
    x2tm_s = nc.dram_tensor("x2tm_s", [TOK, D], F32, kind=kind_dbg).ap()
    x2fm_s = nc.dram_tensor("x2fm_s", [SEQS * D, T], BF16, kind="Internal").ap()
    wqp_s = nc.dram_tensor("wqp_s", [FF, D], BF16, kind=kind_dbg).ap()
    t3_s = nc.dram_tensor("t3_s", [TOK, D], F32, kind="Internal").ap()

    from contextlib import ExitStack
    with ExitStack() as stack:
        E = stack.enter_context
        AB = E(nc.sbuf_tensor("arena_b", [128, 93528], BF16))
        AFR = E(nc.sbuf_tensor("arena_f", [128, 6296], F32))
        ps_qk0 = E(nc.psum_tensor("ps_qk0", [128, 512], F32))
        ps_qk1 = E(nc.psum_tensor("ps_qk1", [128, 512], F32))
        ps_wide = E(nc.psum_tensor("ps_wide", [128, 1024], F32))
        ps_c = E(nc.psum_tensor("ps_c", [128, 512], F32))
        ps_d = E(nc.psum_tensor("ps_d", [128, 512], F32))
        ps_e = E(nc.psum_tensor("ps_e", [128, 1024], BF16))
        ps_7 = E(nc.psum_tensor("ps_7", [128, 512], F32))
        semnames = list(ENGINES) + [f"sd{i}" for i in range(8)] + [f"gd{i}" for i in range(8)]
        sems = {s: E(nc.semaphore(f"sem_{s}")) for s in semnames}
        block = E(nc.Block())

        P = Prog()

        WQFC_O = 0
        ROWP_O = 32768
        BVROW_O = ROWP_O
        BPROW_O = ROWP_O
        UROW_O = ROWP_O
        CROW_O = 33792
        ONEROW_O = 34816
        IDENTB_O = 34944
        ONECOL_O = 35072
        A0 = 35088
        WQK_O = A0
        WV_O = WQK_O + 16384
        WPROJ_O = WV_O + 8192
        XFM_O = WPROJ_O + 8192
        QFM_O = XFM_O + 8192
        KFM_O = QFM_O + 4096
        VTM_O = KFM_O + 4096
        KTM_O = VTM_O + 1024
        AT_O = KTM_O + 1024
        YTM_O = AT_O + 1024
        YFM_O = YTM_O + 1024
        X2FMB_O = YFM_O + 1024
        WQSTG_O = X2FMB_O + 1024
        BSTG_O = WQSTG_O + 512
        ESTG_O = BSTG_O + 512
        KVB_O = ESTG_O + 1024
        MASK4B_O = KVB_O + 520
        WBBF2_O = MASK4B_O + 512
        ABSZ = WBBF2_O + 64
        WQPT_O = A0
        HFM_O = WQPT_O + 32768
        X2FMBB_O = HFM_O + 16384
        WQPSTG_O = X2FMBB_O + 4096
        WQPSTG2_O = WQPSTG_O + 1024
        assert WQPSTG2_O + 1024 <= ABSZ

        MASKF_O = 0
        IDENTF_O = 128
        BQK_O = 256
        BQK1_O = 272
        LNW_O = 288
        LNB_O = 320
        GACC_O = 352
        THRP_O = 416
        GSUM_O = 420
        BNS_O = 432
        AGG_O = 444
        VB_O = 446
        RB_O = 454
        DSTG_O = 462
        RECS_O = 466
        NM_O = 482
        ONECF_O = 486
        R1_O = 487
        MU2_O = 489
        NM2_O = 493
        SS_O = 497
        CONSTS_O = 501
        ONEROWF_O = 512
        AF0 = 656
        XFMSTG_O = AF0
        XFMSTG2_O = AF0 + 512
        XTM_O = AF0 + 1024
        X2TM_O = XTM_O + 1024
        WSTG_O = X2TM_O + 1024
        KVST_O = WSTG_O + 1024
        T1_O = KVST_O + 520
        assert T1_O + 1024 <= 6296
        X2TMB_O = AF0
        T3O_O = AF0 + 1024
        T3I_O = T3O_O + 1024
        OUTS_O = T3I_O + 1024
        UBC_O = OUTS_O + 1024
        assert UBC_O + 1024 <= 6296

        def b(po, off, n):
            return AB[po[0]:po[1], off:off + n]

        def f(po, off, n):
            return AFR[po[0]:po[1], off:off + n]

        FULL = (0, 128)

        Bu = lambda: Buf()
        bufs = {}
        def B_(name):
            if name not in bufs:
                bufs[name] = Buf()
            return bufs[name]
        pq0, pq1, pw0, pw1, pc, pd, pe, p7 = (Buf() for _ in range(8))

        def dma(engine, out_ap, in_ap, reads=(), writes=(), deps=()):
            return P.add_dma(engine, lambda e, o=out_ap, i=in_ap: e.dma_start(out=o, in_=i),
                             reads=reads, writes=writes, deps=deps)

        iotaj = B_("iotaj"); iotap = B_("iotap")
        P.add("gpsimd", lambda e: e.iota(f(FULL, XTM_O, 128), pattern=[[1, 128]], base=0,
                                         channel_multiplier=0, allow_small_or_imprecise_dtypes=True),
              writes=[iotaj])
        P.add("gpsimd", lambda e: e.iota(f(FULL, X2TM_O, 128), pattern=[[0, 128]], base=0,
                                         channel_multiplier=1, allow_small_or_imprecise_dtypes=True),
              writes=[iotap])
        identb = B_("identb"); identf = B_("identf"); maskf = B_("maskf")
        P.add("vector", lambda e: e.tensor_tensor(out=b(FULL, IDENTB_O, 128), in0=f(FULL, XTM_O, 128),
                                                  in1=f(FULL, X2TM_O, 128), op=OP.is_equal),
              reads=[iotaj, iotap], writes=[identb])
        P.add("vector", lambda e: e.tensor_copy(out=f(FULL, IDENTF_O, 128), in_=b(FULL, IDENTB_O, 128)),
              reads=[identb], writes=[identf])
        P.add("vector", lambda e: e.tensor_tensor(out=f(FULL, MASKF_O, 128), in0=f(FULL, X2TM_O, 128),
                                                  in1=f(FULL, XTM_O, 128), op=OP.is_le),
              reads=[iotaj, iotap], writes=[maskf])
        mask4 = B_("mask4")
        for q4 in range(4):
            P.add("vector", lambda e, q4=q4: e.tensor_copy(out=b(FULL, MASK4B_O + q4 * 128, 128),
                                                           in_=f(FULL, MASKF_O, 128)),
                  reads=[maskf], writes=[mask4])
        ones = B_("ones")
        P.add("vector", lambda e: e.memset(b(FULL, ONECOL_O, 1), 1.0), writes=[ones])
        P.add("vector", lambda e: e.memset(AB[0:1, ONEROW_O:ONEROW_O + 128], 1.0), writes=[ones])
        P.add("vector", lambda e: e.memset(AB[32:33, ONEROW_O:ONEROW_O + 128], 1.0), writes=[ones])
        P.add("vector", lambda e: e.memset(AB[64:65, ONEROW_O:ONEROW_O + 128], 1.0), writes=[ones])
        P.add("vector", lambda e: e.memset(f((0, 1), ONEROWF_O, 128), 1.0), writes=[ones])
        P.add("vector", lambda e: e.memset(f(FULL, ONECF_O, 1), 1.0), writes=[ones])
        consts = B_("consts")
        NWT = float(D * FF)
        for idx, val in enumerate([LN_EPS, -0.5, 0.5 / NWT, 0.5 * ABS_EPS, -1.0, MIX_EPS]):
            P.add("vector", lambda e, idx=idx, val=val: e.memset(f(FULL, CONSTS_O + idx, 1), val),
                  writes=[consts])
        C_EPS, C_NEGH, C_HALFNW, C_HALFEPS, C_NEGONE, C_MEPS = (CONSTS_O + i for i in range(6))
        kv0 = B_("kv0"); kv1 = B_("kv1"); kvb0 = B_("kvb0"); kvb1 = B_("kvb1")
        P.add("vector", lambda e: e.memset(f(FULL, KVST_O, 520), 0.0), writes=[kv0])
        P.add("vector", lambda e: e.memset(b(FULL, KVB_O, 520), 0.0), writes=[kvb0])

        bqk = B_("bqk"); rows = B_("rows"); lnwb = B_("lnwb")
        dma("sync", f(FULL, BQK_O, 16), bqk_d, writes=[bqk])
        P.add("vector", lambda e: e.tensor_scalar(out=f(FULL, BQK1_O, 16), in0=f(FULL, BQK_O, 16),
                                                  scalar1=1.0, scalar2=None, op0=OP.add),
              reads=[bqk], writes=[bqk])
        t1b = B_("t1b")
        dma("sync", f((0, 1), T1_O, 1024), bv_d, writes=[t1b])
        P.add("vector", lambda e: e.tensor_copy(out=AB[0:1, BVROW_O:BVROW_O + 1024],
                                                in_=f((0, 1), T1_O, 1024)),
              reads=[t1b], writes=[rows])
        dma("sync", AFR[32:33, T1_O:T1_O + 1024], bproj_d, reads=[rows], writes=[t1b])
        P.add("vector", lambda e: e.tensor_copy(out=AB[32:33, BPROW_O:BPROW_O + 1024],
                                                in_=AFR[32:33, T1_O:T1_O + 1024]),
              reads=[t1b], writes=[rows])
        dma("sync", f(FULL, LNW_O, 32), lnw_d, writes=[lnwb])
        dma("sync", f(FULL, LNB_O, 32), lnb_d, writes=[lnwb])
        wbbf = B_("wbbf")
        P.add("vector", lambda e: e.tensor_copy(out=AB[:, WBBF2_O:WBBF2_O + 64][:, 0:64:2],
                                                in_=f(FULL, LNW_O, 32)),
              reads=[lnwb], writes=[wbbf])
        P.add("vector", lambda e: e.tensor_copy(out=AB[:, WBBF2_O + 1:WBBF2_O + 64][:, 0:63:2],
                                                in_=f(FULL, LNB_O, 32)),
              reads=[lnwb], writes=[wbbf])

        wqk = [B_(f"wqk{k}") for k in range(KC)]
        wv = [B_(f"wv{k}") for k in range(KC)]
        wpj = [B_(f"wpj{k}") for k in range(KC)]
        wstg = [B_("wstg0"), B_("wstg1")]
        for k in range(KC):
            for j in range(4):
                s = wstg[(k * 8 + j) % 2]
                dma("sync", f(FULL, WSTG_O + ((k * 8 + j) % 2) * 512, 512),
                    wattn_d[k * 128:(k + 1) * 128, j * 512:(j + 1) * 512], writes=[s])
                P.add("vector", lambda e, k=k, j=j, m=(k * 8 + j) % 2:
                      e.tensor_copy(out=b(FULL, WQK_O + k * 2048 + j * 512, 512),
                                    in_=f(FULL, WSTG_O + m * 512, 512)),
                      reads=[s], writes=[wqk[k]])
            for j in range(2):
                s = wstg[(k * 8 + 4 + j) % 2]
                dma("sync", f(FULL, WSTG_O + ((k * 8 + 4 + j) % 2) * 512, 512),
                    wattn_d[k * 128:(k + 1) * 128, 2048 + j * 512:2048 + (j + 1) * 512], writes=[s])
                P.add("vector", lambda e, k=k, j=j, m=(k * 8 + 4 + j) % 2:
                      e.tensor_copy(out=b(FULL, WV_O + k * 1024 + j * 512, 512),
                                    in_=f(FULL, WSTG_O + m * 512, 512)),
                      reads=[s], writes=[wv[k]])
            for j in range(2):
                s = wstg[(k * 8 + 6 + j) % 2]
                dma("sync", f(FULL, WSTG_O + ((k * 8 + 6 + j) % 2) * 512, 512),
                    wproj_d[k * 128:(k + 1) * 128, j * 512:(j + 1) * 512], writes=[s])
                P.add("vector", lambda e, k=k, j=j, m=(k * 8 + 6 + j) % 2:
                      e.tensor_copy(out=b(FULL, WPROJ_O + k * 1024 + j * 512, 512),
                                    in_=f(FULL, WSTG_O + m * 512, 512)),
                      reads=[s], writes=[wpj[k]])

        xfm = [[B_(f"xfm{bi}_{k}") for k in range(KC)] for bi in range(2)]
        xstg = [B_("xstg0"), B_("xstg1")]

        def load_xfm(sc, dep_ops=()):
            seq, t0 = sc // 4, (sc % 4) * SCT
            bi = sc % 2
            for k in range(KC):
                s = xstg[k % 2]
                so = XFMSTG_O if k % 2 == 0 else XFMSTG2_O
                dma("sync", f(FULL, so, 512),
                    x_fm_d[seq * D + k * 128: seq * D + (k + 1) * 128, t0:t0 + SCT],
                    writes=[s], deps=dep_ops)
                P.add("vector", lambda e, k=k, so=so, bi=bi:
                      e.tensor_copy(out=b(FULL, XFM_O + bi * 4096 + k * 512, 512),
                                    in_=f(FULL, so, 512)),
                      reads=[s], writes=[xfm[bi][k]])

        load_xfm(0)

        qfm = [B_(f"qfm{m}") for m in range(8)]
        kfm = [B_(f"kfm{m}") for m in range(8)]
        estg = [B_("estg0"), B_("estg1")]

        def qk_mms(sc):
            bi = sc % 2
            for m in range(16):
                pq, pqb = (ps_qk0, pq0) if m % 2 == 0 else (ps_qk1, pq1)
                def mmfn(e, m=m, pq=pq, bi=bi):
                    for k in range(KC):
                        ins = e.matmul(pq[:, 0:512], b(FULL, WQK_O + k * 2048 + m * 128, 128),
                                       b(FULL, XFM_O + bi * 4096 + k * 512, 512),
                                       start=(k == 0), stop=(k == KC - 1))
                    return ins
                hmm = P.add("tensor", mmfn,
                            reads=[wqk[k] for k in range(KC)] + [xfm[bi][k] for k in range(KC)],
                            writes=[pqb])
                dest_o = (QFM_O + m * 512) if m < 8 else (KFM_O + (m - 8) * 512)
                dest_b = qfm[m] if m < 8 else kfm[m - 8]
                es = estg[m % 2]
                P.add("scalar", lambda e, m=m, pq=pq, eo=ESTG_O + (m % 2) * 512:
                      e.activation(out=b(FULL, eo, 512), in_=pq[:, 0:512], func=AF.Exp,
                                   bias=f(FULL, BQK_O + m, 1), scale=1.0),
                      reads=[pqb, bqk], writes=[es])
                P.add("scalar", lambda e, pq=pq, do=dest_o, m=m:
                      e.activation(out=b(FULL, do, 512), in_=pq[:, 0:512], func=AF.Relu,
                                   bias=f(FULL, BQK_O + m, 1), scale=1.0),
                      reads=[pqb, bqk], writes=[dest_b])
                P.add("vector", lambda e, do=dest_o, eo=ESTG_O + (m % 2) * 512:
                      e.scalar_tensor_tensor(out=b(FULL, do, 512), in0=b(FULL, do, 512),
                                             scalar=1.0, in1=b(FULL, eo, 512),
                                             op0=OP.add, op1=OP.min),
                      reads=[es, dest_b], writes=[dest_b])

        vtm = B_("vtm")
        ktm = B_("ktm")
        at = [B_("ath0"), B_("ath1"), B_("ath0"), B_("ath1")]
        ytm = B_("ytm")
        yfm = B_("yfm")
        x2tm = B_("x2tm")
        x2fmb = B_("x2fmb")
        xtm = B_("xtm")
        t1 = B_("t1")
        stats = B_("stats")
        recs = B_("recs")

        def chunk_ln1(cg):
            if ON('ln1'):
                P.add("vector", lambda e: e.tensor_tensor(out=f(FULL, T1_O, 1024), in0=ps_wide[:, 0:1024],
                                                          in1=f(FULL, XTM_O, 1024), op=OP.add),
                      reads=[pw0, pw1, xtm], writes=[t1])
                P.add("vector", lambda e: e.bn_stats(out=f(FULL, BNS_O, 6), in_=f(FULL, T1_O, 512)),
                      reads=[t1], writes=[stats])
                P.add("vector", lambda e: e.bn_stats(out=f(FULL, BNS_O + 6, 6), in_=f(FULL, T1_O + 512, 512)),
                      reads=[t1], writes=[stats])
                P.add("vector", lambda e: e.bn_aggr(out=f(FULL, AGG_O, 2), in_=f(FULL, BNS_O, 12)),
                      reads=[stats], writes=[stats])
                P.add("scalar", lambda e: e.activation(out=f(FULL, R1_O, 1), in_=f(FULL, AGG_O + 1, 1),
                                                       func=AF.Ln, bias=f(FULL, C_EPS, 1), scale=1.0),
                      reads=[stats, consts], writes=[stats])
                P.add("scalar", lambda e: e.activation(out=f(FULL, R1_O + 1, 1), in_=f(FULL, R1_O, 1),
                                                       func=AF.Exp, bias=0.0, scale=f(FULL, C_NEGH, 1)),
                      reads=[stats, consts], writes=[stats])
                P.add("vector", lambda e: e.tensor_scalar(out=f(FULL, NM_O, 1), in0=f(FULL, AGG_O, 1),
                                                          scalar1=f(FULL, R1_O + 1, 1), scalar2=-1.0,
                                                          op0=OP.mult, op1=OP.mult),
                      reads=[stats], writes=[stats])
                P.add("scalar", lambda e: e.activation(out=f(FULL, X2TM_O, 1024), in_=f(FULL, T1_O, 1024),
                                                       func=AF.Identity, bias=f(FULL, NM_O, 1),
                                                       scale=f(FULL, R1_O + 1, 1)),
                      reads=[t1, stats], writes=[x2tm])
                dma("gpsimd", x2tm_s[cg * 128:(cg + 1) * 128, :], f(FULL, X2TM_O, 1024), reads=[x2tm])


        def chunk_x2t(cg):
            if not ON('x2t'):
                return
            for r in range(2):
                def x2t(e, r=r):
                    for j in range(4):
                        fc_ = 4 * r + j
                        ins = e.transpose(ps_7[:, j * 128:(j + 1) * 128],
                                          f(FULL, X2TM_O + fc_ * 128, 128), f(FULL, IDENTF_O, 128))
                    return ins
                P.add("tensor", x2t, reads=[x2tm, identf], writes=[p7])
                P.add("scalar", lambda e, r=r:
                      e.copy(out=b(FULL, X2FMB_O + r * 512, 512), in_=ps_7[:, 0:512]),
                      reads=[p7], writes=[x2fmb])
            seq, t0 = cg // 16, (cg % 16) * 128
            for j in range(KC):
                dma("gpsimd", x2fm_s[seq * D + j * 128: seq * D + (j + 1) * 128, t0:t0 + 128],
                    AB[:, X2FMB_O + j * 128:X2FMB_O + (j + 1) * 128], reads=[x2fmb])

        def chunk_attn(cg, filler=None, pre=None):
            sc, c = cg // CPS, cg % CPS
            seq = sc // 4
            heads_round = [[0, 2, 4, 6], [1, 3, 5, 7], [8, 10, 12, 14], [9, 11, 13, 15]]
            kvr = kvw = kv0
            kvbr = kvbw = kvb0
            kvr_o = kvw_o = KVST_O
            kvbr_o = kvbw_o = KVB_O
            seq_last = (cg % 16) == 15

            if ON('ktr'):
                def ktr(e):
                    for fch in range(KC):
                        ins = e.transpose(ps_e[:, fch * 128:(fch + 1) * 128],
                                          AB[:, KFM_O + fch * 512 + c * 128:
                                             KFM_O + fch * 512 + (c + 1) * 128],
                                          b(FULL, IDENTB_O, 128))
                    return ins
                P.add("tensor", ktr, reads=[kfm[m] for m in range(8)] + [identb], writes=[pe])
                P.add("scalar", lambda e: e.copy(out=b(FULL, KTM_O, 1024), in_=ps_e[:, 0:1024]),
                      reads=[pe], writes=[ktm])

            mixbank = [(ps_c, pc), (ps_d, pd), (ps_7, p7), (ps_c, pc)]

            def emit_st(r):
                if not ON('st'):
                    return
                psr, pbr = mixbank[r]
                def stfn(e, r=r, psr=psr, c=c):
                    for j in range(4):
                        h = heads_round[r][j]
                        fch, half = h // 2, h % 2
                        ksl = AB[64 * half:64 * half + 64,
                                 KFM_O + fch * 512 + c * 128: KFM_O + fch * 512 + (c + 1) * 128]
                        qsl = AB[64 * half:64 * half + 64,
                                 QFM_O + fch * 512 + c * 128: QFM_O + fch * 512 + (c + 1) * 128]
                        ins = e.matmul(psr[:, j * 128:(j + 1) * 128], ksl, qsl, start=True, stop=True)
                    return ins
                P.add("tensor", stfn, reads=[qfm[m] for m in range(8)] + [kfm[m] for m in range(8)],
                      writes=[pbr])
                P.add("vector", lambda e, r=r, psr=psr:
                      e.tensor_tensor(out=b(FULL, AT_O + (r % 2) * 512, 512), in0=psr[:, 0:512],
                                  in1=b(FULL, MASK4B_O, 512), op=OP.mult),
                      reads=[pbr, mask4], writes=[at[r]])

            def emit_num(r):
                if not ON('num'):
                    return
                psr, pbr = mixbank[r]
                def numfn(e, r=r, psr=psr, c=c, kvbr_o=kvbr_o):
                    for j in range(4):
                        h = heads_round[r][j]
                        fch, half = h // 2, h % 2
                        atsl = b(FULL, AT_O + (r % 2) * 512 + j * 128, 128)
                        qsl = AB[64 * half:64 * half + 64,
                                 QFM_O + fch * 512 + c * 128: QFM_O + fch * 512 + (c + 1) * 128]
                        kvo = kvbr_o + (h // 2) * 65
                        kv_num = AB[64 * half:64 * half + 64, kvo:kvo + 64]
                        kv_ks = AB[64 * half:64 * half + 64, kvo + 64:kvo + 65]
                        e.matmul(psr[:, j * 65:j * 65 + 64], atsl, b(FULL, VTM_O + h * 64, 64),
                                 start=(j == 0), stop=False)
                        e.matmul(psr[:, j * 65 + 64:j * 65 + 65], atsl, b(FULL, ONECOL_O, 1),
                                 start=False, stop=False)
                        e.matmul(psr[:, j * 65:j * 65 + 64], qsl, kv_num, start=False, stop=False)
                        ins = e.matmul(psr[:, j * 65 + 64:j * 65 + 65], qsl, kv_ks,
                                   start=False, stop=(j == 3))
                    return ins
                P.add("tensor", numfn, reads=[at[r], vtm, ones, kvbr] + [qfm[m] for m in range(8)],
                      writes=[pbr])
                P.add("scalar", lambda e, r=r, psr=psr:
                      e.activation(out=f(FULL, DSTG_O, 4), in_=psr[:, 64:64 + 3 * 65 + 1:65],
                                   func=AF.Ln, bias=f(FULL, C_MEPS, 1), scale=1.0),
                      reads=[pbr, at[r], consts], writes=[stats])
                P.add("scalar", lambda e, r=r: e.activation(out=f(FULL, RECS_O + 4 * r, 4),
                                                            in_=f(FULL, DSTG_O, 4), func=AF.Exp,
                                                            bias=0.0, scale=f(FULL, C_NEGONE, 1)),
                      reads=[stats, consts], writes=[recs])
                def ysc(e, r=r, psr=psr):
                    for j in range(4):
                        h = heads_round[r][j]
                        ins = e.activation(out=b(FULL, YTM_O + h * 64, 64),
                                       in_=psr[:, j * 65:j * 65 + 64], func=AF.Copy,
                                       bias=0.0, scale=f(FULL, RECS_O + 4 * r + j, 1))
                    return ins
                P.add("scalar", ysc, reads=[pbr, recs], writes=[ytm])

            if ON('st'):
                emit_st(0)
                emit_st(1)
            if ON('v'):
                for half in range(2):
                    psh = ps_qk0[:, 0:512] if half == 0 else ps_qk1[:, 0:512]
                    pb = pq0 if half == 0 else pq1
                    bi = sc % 2
                    def vfn(e, half=half, psh=psh, c=c, bi=bi):
                        for k in range(KC):
                            xo = XFM_O + bi * 4096 + k * 512 + c * 128
                            e.matmul(psh, AB[:, xo:xo + 128],
                                     b(FULL, WV_O + k * 1024 + half * 512, 512),
                                     start=(k == 0), stop=False)
                        return e.matmul(psh, b((0, 1), ONEROW_O, 128),
                                        b((0, 1), BVROW_O + half * 512, 512), start=False, stop=True)
                    P.add("tensor", vfn, reads=[xfm[sc % 2][k] for k in range(KC)]
                          + [wv[k] for k in range(KC)] + [rows, ones], writes=[pb])
                    P.add("vector", lambda e, half=half, psh=psh:
                          e.tensor_copy(out=b(FULL, VTM_O + half * 512, 512), in_=psh),
                          reads=[pb], writes=[vtm])
            if cg > 0 and ON('ln1'):
                chunk_ln1(cg - 1)
            if ON('proj'):
                dma("sync", f(FULL, XTM_O, 1024), x_tm_d[cg * 128:(cg + 1) * 128, :], writes=[xtm])

            if ON('st'):
                emit_num(0)
                emit_st(2)
                emit_num(1)
                emit_st(3)
                emit_num(2)
                emit_num(3)

            if ON('kv'):
                if not seq_last:
                    for r in range(4):
                        psr, pbr = mixbank[(r + 1) % 4]
                        def kvfn(e, r=r, psr=psr):
                            for half in range(2):
                                bp = 64 * half
                                heads = [4 * r + half, 4 * r + 2 + half]
                                for idx, h in enumerate(heads):
                                    ktsl = b(FULL, KTM_O + h * 64, 64)
                                    col = (h // 2) % 2
                                    e.matmul(psr[bp:bp + 64, col * 65:col * 65 + 64],
                                             ktsl, b(FULL, VTM_O + h * 64, 64),
                                             start=(idx == 0), stop=False)
                                    ins = e.matmul(psr[bp:bp + 64, col * 65 + 64:col * 65 + 65],
                                                   ktsl, b(FULL, ONECOL_O, 1), start=False,
                                                   stop=(idx == 1))
                            return ins
                        P.add("tensor", kvfn, reads=[ktm, vtm, ones], writes=[pbr])
                        P.add("vector", lambda e, r=r, psr=psr, kvr_o=kvr_o, kvw_o=kvw_o:
                              e.scalar_tensor_tensor(out=f(FULL, kvw_o + r * 130, 130),
                                                     in0=psr[:, 0:130], scalar=1.0,
                                                     in1=f(FULL, kvr_o + r * 130, 130),
                                                     op0=OP.mult, op1=OP.add),
                              reads=[pbr, kvr], writes=[kvw])
                    if r == 3:
                        P.add("scalar", lambda e, kvw_o=kvw_o, kvbw_o=kvbw_o:
                              e.copy(out=b(FULL, kvbw_o, 520), in_=f(FULL, kvw_o, 520)),
                              reads=[kvw], writes=[kvbw])
                else:
                    P.add("vector", lambda e, kvw_o=kvw_o: e.memset(f(FULL, kvw_o, 520), 0.0),
                          reads=[kvr], writes=[kvw])
                    P.add("vector", lambda e, kvbw_o=kvbw_o: e.memset(b(FULL, kvbw_o, 520), 0.0),
                          reads=[kvbr], writes=[kvbw])

            if pre is not None:
                pre()
            if filler is not None:
                filler()

            if ON('yt'):
                def ytr(e):
                    for fch in range(KC):
                        ins = e.transpose(ps_e[:, fch * 128:(fch + 1) * 128],
                                          b(FULL, YTM_O + fch * 128, 128),
                                          b(FULL, IDENTB_O, 128))
                    return ins
                P.add("tensor", ytr, reads=[ytm, identb], writes=[pe])
                P.add("scalar", lambda e: e.copy(out=b(FULL, YFM_O, 1024), in_=ps_e[:, 0:1024]),
                      reads=[pe], writes=[yfm])

            if ON('proj'):
                pass

                for half in range(2):
                    psh = ps_wide[:, half * 512:(half + 1) * 512]
                    pb = pw0 if half == 0 else pw1
                    def pjfn(e, half=half, psh=psh):
                        for k in range(KC):
                            e.matmul(psh, b(FULL, YFM_O + k * 128, 128),
                                     b(FULL, WPROJ_O + k * 1024 + half * 512, 512),
                                     start=(k == 0), stop=False)
                        return e.matmul(psh, AB[32:33, ONEROW_O:ONEROW_O + 128],
                                        AB[32:33, BPROW_O + half * 512:BPROW_O + half * 512 + 512],
                                        start=False, stop=True)
                    P.add("tensor", pjfn, reads=[yfm, rows, ones] + [wpj[k] for k in range(KC)],
                          writes=[pb])

        gacc = B_("gacc"); thrp = B_("thrp"); gs = B_("gs"); throw_b = B_("throw")
        bstg = B_("bstg"); wqstg = B_("wqstg")
        wqfc = [B_(f"wqfc{k}") for k in range(KC)]
        NW = D * FF

        def gamma_slice(widx, i):
            src = wfct_d if widx == 0 else wpft_d
            sl = src[(i // 8) * 128:(i // 8) * 128 + 128, (i % 8) * 512:(i % 8) * 512 + 512] \
                if widx == 0 else src[(i // 2) * 128:(i // 2) * 128 + 128,
                                      (i % 2) * 512:(i % 2) * 512 + 512]
            sb = wstg[i % 2]
            dma("sync", f(FULL, WSTG_O + (i % 2) * 512, 512), sl, writes=[sb])
            es = estg[i % 2]
            P.add("scalar", lambda e, i=i: e.activation(out=b(FULL, ESTG_O + (i % 2) * 512, 512),
                                                        in_=f(FULL, WSTG_O + (i % 2) * 512, 512),
                                                        func=AF.Abs,
                                                        accum_out=f(FULL, GACC_O + i, 1)),
                  reads=[sb], writes=[es, gacc])

        def gamma_fin(widx):
            P.add("vector", lambda e: e.tensor_reduce(out=f(FULL, GSUM_O, 1), in_=f(FULL, GACC_O, 64),
                                                      axis=mybir.AxisListType.X, op=OP.add),
                  reads=[gacc], writes=[gs])
            P.add("tensor", lambda e: e.matmul(ps_7[0:1, 0:1], f(FULL, GSUM_O, 1),
                                               f(FULL, ONECF_O, 1), start=True, stop=True),
                  reads=[gs, ones], writes=[p7])
            P.add("vector", lambda e: e.tensor_copy(out=f((0, 1), ONEROWF_O + 129, 1),
                                                    in_=ps_7[0:1, 0:1]),
                  reads=[p7], writes=[gs])
            P.add("scalar", lambda e: e.activation(out=f((0, 1), ONEROWF_O + 128, 1),
                                                   in_=f((0, 1), ONEROWF_O + 129, 1),
                                                   func=AF.Identity,
                                                   bias=f((0, 1), C_HALFEPS, 1),
                                                   scale=f((0, 1), C_HALFNW, 1)),
                  reads=[gs, consts], writes=[throw_b])
            P.add("scalar", lambda e: e.activation(out=f((0, 1), ONEROWF_O + 129, 1),
                                                   in_=f((0, 1), ONEROWF_O + 128, 1),
                                                   func=AF.Identity, bias=0.0,
                                                   scale=f((0, 1), C_NEGONE, 1)),
                  reads=[throw_b, consts], writes=[throw_b])
            P.add("tensor", lambda e: e.matmul(ps_7[:, 0:2], f((0, 1), ONEROWF_O, 128),
                                               f((0, 1), ONEROWF_O + 128, 2), start=True, stop=True),
                  reads=[throw_b, ones], writes=[p7])
            P.add("vector", lambda e, widx=widx: e.tensor_copy(out=f(FULL, THRP_O + 2 * widx, 2),
                                                               in_=ps_7[:, 0:2]),
                  reads=[p7], writes=[thrp])

        def quant_slice(widx, i):
            src = wfct_d if widx == 0 else wpft_d
            if widx == 0:
                sl = src[(i // 8) * 128:(i // 8) * 128 + 128, (i % 8) * 512:(i % 8) * 512 + 512]
            else:
                sl = src[(i // 2) * 128:(i // 2) * 128 + 128, (i % 2) * 512:(i % 2) * 512 + 512]
            sb = wstg[i % 2]
            dma("sync", f(FULL, WSTG_O + (i % 2) * 512, 512), sl, writes=[sb])
            P.add("vector", lambda e, i=i, widx=widx:
                  e.tensor_scalar(out=b(FULL, BSTG_O, 512), in0=f(FULL, WSTG_O + (i % 2) * 512, 512),
                                  scalar1=f(FULL, THRP_O + 2 * widx + 1, 1), scalar2=None,
                                  op0=OP.is_le),
                  reads=[sb, thrp], writes=[bstg])
            if widx == 0:
                dst_o = WQFC_O + (i // 8) * 4096 + (i % 8) * 512
                dst_b = wqfc[i // 8]
            else:
                dst_o = WQSTG_O
                dst_b = wqstg
            P.add("vector", lambda e, i=i, widx=widx, dst_o=dst_o:
                  e.scalar_tensor_tensor(out=b(FULL, dst_o, 512),
                                         in0=f(FULL, WSTG_O + (i % 2) * 512, 512),
                                         scalar=f(FULL, THRP_O + 2 * widx, 1),
                                         in1=b(FULL, BSTG_O, 512), op0=OP.is_ge, op1=OP.subtract),
                  reads=[sb, bstg, thrp], writes=[dst_b])
            if widx == 1:
                hc_, half = i // 2, i % 2
                dma("gpsimd", wqp_s[hc_ * 128:(hc_ + 1) * 128, half * 512:(half + 1) * 512],
                    b(FULL, WQSTG_O, 512), reads=[wqstg])

        def a_tasks_for_chunk(cg):
            sc, c = cg // 4, cg % 4
            i0 = (cg % 8) * 8
            if sc in (0, 1):
                for i in range(i0, i0 + 8):
                    gamma_slice(0, i)
                if cg == 7:
                    gamma_fin(0)
            elif sc in (2, 3):
                for i in range(i0, i0 + 8):
                    gamma_slice(1, i)
                if cg == 15:
                    gamma_fin(1)
            elif sc in (4, 5):
                for i in range(i0, i0 + 8):
                    quant_slice(0, i)
            else:
                j0 = (cg % 4) * 8 + (0 if sc == 6 else 1) * 32
                idxs = [2 * jj for jj in range(j0 % 32, j0 % 32 + 8)] if sc == 6 \
                    else [2 * jj + 1 for jj in range(j0 % 32, j0 % 32 + 8)]
                for i in idxs:
                    quant_slice(1, i)

        for sc in range(n_sc):
            if sc == 0 and ON('qk'):
                qk_mms(0)
            if sc + 1 < n_sc:
                load_xfm(sc + 1)
            for c in range(CPS):
                cg = sc * CPS + c
                pre = (lambda g=cg - 1: chunk_x2t(g)) if cg > 0 else None
                chunk_attn(cg, pre=pre)
                if ON('tasks'):
                    a_tasks_for_chunk(cg)
            if sc + 1 < n_sc and ON('qk'):
                qk_mms(sc + 1)
            if sc == n_sc - 1:
                if ON('ln1'):
                    chunk_ln1(n_sc * CPS - 1)
                chunk_x2t(n_sc * CPS - 1)

        ucrows = B_("ucrows")
        ucstg = B_("ucstg")

        if not phase_b:
            zb = B_("zerob")
            P.add("vector", lambda e: e.memset(f(FULL, T1_O, 1024), 0.0), writes=[zb])
            for cgz in range(TOK // 128):
                dma("gpsimd", out_d[cgz * 128:(cgz + 1) * 128, :], f(FULL, T1_O, 1024),
                    reads=[zb])
            P.emit(block, sems)
            return nc
        barrier = P.barrier_handles()

        wqpt = [Buf(init=barrier) for _ in range(HC)]
        wqps = Buf(init=barrier)
        wqps2 = Buf(init=barrier)
        hfm = [Buf(init=barrier) for _ in range(HC)]
        x2fmB = [Buf(init=barrier) for _ in range(KC)]
        x2tmB = Buf(init=barrier)
        t3o = Buf(init=barrier)
        t3i = Buf(init=barrier)
        outs_b = Buf(init=barrier)
        ubc = Buf(init=barrier)
        vb = Buf(init=barrier)
        rb = Buf(init=barrier)
        statB = Buf(init=barrier)

        def wqpt_dma(hc_):
            dma("sync", b(FULL, WQPSTG2_O if hc_ % 2 else WQPSTG_O, 1024),
                wqp_s[hc_ * 128:(hc_ + 1) * 128, :],
                writes=[wqps2 if hc_ % 2 else wqps])

        def wqpt_work(hc_):
            so = WQPSTG2_O if hc_ % 2 else WQPSTG_O
            sb_ = wqps2 if hc_ % 2 else wqps
            for half in range(2):
                P.add("tensor", lambda e, hc_=hc_, half=half, so=so:
                      e.matmul(ps_7[64 * half:64 * half + 2, 0:512],
                               AB[:, WBBF2_O + 2 * hc_:WBBF2_O + 2 * hc_ + 2],
                               b(FULL, so + half * 512, 512),
                               start=(hc_ == 0), stop=(hc_ == HC - 1)),
                      reads=[sb_, wbbf], writes=[p7])
            P.add("vector", lambda e, hc_=hc_, so=so:
                  e.tensor_scalar(out=b(FULL, WQPT_O + hc_ * 1024, 1024), in0=b(FULL, so, 1024),
                                  scalar1=f(FULL, LNW_O + hc_, 1), scalar2=None, op0=OP.mult),
                  reads=[sb_, lnwb], writes=[wqpt[hc_]])

        def uc_finish():
            P.add("vector", lambda e: e.tensor_copy(out=AB[0:2, WQPSTG_O:WQPSTG_O + 512],
                                                    in_=ps_7[0:2, 0:512]),
                  reads=[p7], writes=[ucstg, wqps])
            P.add("vector", lambda e: e.tensor_copy(out=AB[64:66, WQPSTG_O + 512:WQPSTG_O + 1024],
                                                    in_=ps_7[64:66, 0:512]),
                  reads=[p7], writes=[ucstg, wqps])
            dma("sync", AB[64:65, UROW_O:UROW_O + 512], AB[0:1, WQPSTG_O:WQPSTG_O + 512],
                reads=[ucstg], writes=[ucrows])
            dma("sync", AB[0:1, CROW_O:CROW_O + 512], AB[1:2, WQPSTG_O:WQPSTG_O + 512],
                reads=[ucstg], writes=[ucrows])
            dma("sync", AB[64:65, UROW_O + 512:UROW_O + 1024],
                AB[64:65, WQPSTG_O + 512:WQPSTG_O + 1024], reads=[ucstg], writes=[ucrows])
            dma("sync", AB[0:1, CROW_O + 512:CROW_O + 1024],
                AB[65:66, WQPSTG_O + 512:WQPSTG_O + 1024], reads=[ucstg], writes=[ucrows])
            for half in range(2):
                P.add("tensor", lambda e, half=half:
                      e.matmul(ps_7[:, 0:512], AB[64:65, ONEROW_O:ONEROW_O + 128],
                               AB[64:65, UROW_O + half * 512:UROW_O + half * 512 + 512],
                               start=True, stop=True),
                      reads=[ucrows, ones], writes=[p7])
                P.add("vector", lambda e, half=half:
                      e.tensor_copy(out=f(FULL, UBC_O + half * 512, 512), in_=ps_7[:, 0:512]),
                      reads=[p7], writes=[ubc])

        fcps = [(ps_qk0[:, 0:512], pq0), (ps_qk1[:, 0:512], pq1), (ps_wide[:, 0:512], pw0), (ps_wide[:, 512:1024], pw1)]

        def load_x2fm(sc):
            seq, t0 = sc // 4, (sc % 4) * SCT
            for k in range(KC):
                dma("sync", b(FULL, X2FMBB_O + k * 512, 512),
                    x2fm_s[seq * D + k * 128: seq * D + (k + 1) * 128, t0:t0 + SCT],
                    writes=[x2fmB[k]])

        load_x2fm(0)

        t3s_bufs = [Buf(init=barrier) for _ in range(TOK // 128)]

        def apply_ln3(cg, rcol):
            dma("sync", f(FULL, T3I_O, 1024), t3_s[cg * 128:(cg + 1) * 128, :],
                reads=[t3s_bufs[cg]], writes=[t3i])
            P.add("vector", lambda e, rcol=rcol:
                  e.tensor_scalar(out=f(FULL, NM_O + 1, 1), in0=f(FULL, MU3_O + rcol - 4, 1),
                                  scalar1=f(FULL, RB_O + rcol, 1), scalar2=-1.0,
                                  op0=OP.mult, op1=OP.mult),
              reads=[rb, statB], writes=[statB])
            P.add("scalar", lambda e, rcol=rcol:
                  e.activation(out=f(FULL, OUTS_O, 1024), in_=f(FULL, T3I_O, 1024),
                               func=AF.Identity,
                               bias=f(FULL, NM_O + 1, 1), scale=f(FULL, RB_O + rcol, 1)),
                  reads=[t3i, rb, statB], writes=[outs_b])
            dma("gpsimd", out_d[cg * 128:(cg + 1) * 128, :], f(FULL, OUTS_O, 1024), reads=[outs_b])

        MU3_O = SS_O

        prev_mu3 = [None] * 4

        for sc in range(n_sc):
            for hc_ in range(HC):
                if sc == 0:
                    wqpt_dma(hc_)
                    if hc_ >= 1:
                        wqpt_work(hc_ - 1)
                psh, pb = fcps[hc_ % 4]
                def fcfn(e, hc_=hc_, psh=psh):
                    for k in range(KC):
                        ins = e.matmul(psh, AB[:, WQFC_O + k * 4096 + hc_ * 128:
                                               WQFC_O + k * 4096 + (hc_ + 1) * 128],
                                       b(FULL, X2FMBB_O + k * 512, 512),
                                       start=(k == 0), stop=(k == KC - 1))
                    return ins
                P.add("tensor", fcfn, reads=[wqfc[k] for k in range(KC)] + [x2fmB[k] for k in range(KC)],
                      writes=[pb])
                P.add("scalar", lambda e, hc_=hc_, psh=psh:
                      e.activation(out=b(FULL, HFM_O + hc_ * 512, 512), in_=psh, func=AF.Gelu),
                      reads=[pb], writes=[hfm[hc_]])

            if sc == 0:
                wqpt_work(HC - 1)
                uc_finish()
            for c in range(CPS):
                def ssfn(e, c=c):
                    for hc_ in range(HC):
                        sl = AB[:, HFM_O + hc_ * 512 + c * 128: HFM_O + hc_ * 512 + (c + 1) * 128]
                        ins = e.matmul(ps_7[:, 0:128], sl, sl, start=(hc_ == 0), stop=False)
                    for hc_ in range(HC):
                        sl = AB[:, HFM_O + hc_ * 512 + c * 128: HFM_O + hc_ * 512 + (c + 1) * 128]
                        ins = e.matmul(ps_7[:, 128:129], sl, b(FULL, ONECOL_O, 1),
                                       start=False, stop=(hc_ == HC - 1))
                    return ins
                P.add("tensor", ssfn, reads=[hfm[i] for i in range(HC)] + [ones], writes=[p7])
                P.add("vector", lambda e: e.tensor_tensor(out=f(FULL, T3O_O, 128),
                                                          in0=ps_7[:, 0:128],
                                                          in1=f(FULL, IDENTF_O, 128), op=OP.mult),
                      reads=[p7, identf], writes=[t3o])
                P.add("vector", lambda e, c=c: e.tensor_reduce(out=f(FULL, R1_O, 1),
                                                               in_=f(FULL, T3O_O, 128),
                                                               axis=mybir.AxisListType.X, op=OP.add),
                      reads=[t3o], writes=[statB])
                P.add("vector", lambda e, c=c: e.tensor_scalar(out=f(FULL, MU2_O + c, 1),
                                                               in0=ps_7[:, 128:129],
                                                               scalar1=1.0 / FF, scalar2=None,
                                                               op0=OP.mult),
                      reads=[p7], writes=[statB])
                P.add("vector", lambda e, c=c:
                      e.tensor_scalar(out=f(FULL, NM_O + 2, 1), in0=f(FULL, MU2_O + c, 1),
                                      scalar1=f(FULL, MU2_O + c, 1), scalar2=None, op0=OP.mult),
                      reads=[statB], writes=[statB])
                P.add("vector", lambda e, c=c:
                      e.scalar_tensor_tensor(out=f(FULL, VB_O + c, 1), in0=f(FULL, R1_O, 1),
                                             scalar=1.0 / FF, in1=f(FULL, NM_O + 2, 1),
                                             op0=OP.mult, op1=OP.subtract),
                      reads=[statB], writes=[vb])

            ncols = 8 if sc > 0 else 4
            P.add("scalar", lambda e, ncols=ncols: e.activation(out=f(FULL, RB_O, ncols),
                                                                in_=f(FULL, VB_O, ncols),
                                                                func=AF.Ln, bias=f(FULL, C_EPS, 1),
                                                                scale=1.0),
                  reads=[vb, consts], writes=[rb])
            P.add("scalar", lambda e, ncols=ncols: e.activation(out=f(FULL, RB_O, ncols),
                                                                in_=f(FULL, RB_O, ncols),
                                                                func=AF.Exp, bias=0.0,
                                                                scale=f(FULL, C_NEGH, 1)),
                  reads=[rb, consts], writes=[rb])

            if sc > 0:
                for c in range(CPS):
                    apply_ln3((sc - 1) * CPS + c, 4 + c)

            for c in range(CPS):
                cg = sc * CPS + c
                dma("sync", f(FULL, X2TMB_O, 1024), x2tm_s[cg * 128:(cg + 1) * 128, :],
                    writes=[x2tmB])
                for half in range(2):
                    psr, pbr = (ps_c, pc) if half == 0 else (ps_d, pd)
                    def gfn(e, c=c, half=half, psr=psr):
                        for hc_ in range(HC):
                            e.matmul(psr[:, 0:512],
                                     AB[:, HFM_O + hc_ * 512 + c * 128: HFM_O + hc_ * 512 + (c + 1) * 128],
                                     b(FULL, WQPT_O + hc_ * 1024 + half * 512, 512),
                                     start=(hc_ == 0), stop=False)
                        return e.matmul(psr[:, 0:512], b((0, 1), ONEROW_O, 128),
                                        b((0, 1), CROW_O + half * 512, 512), start=False, stop=True)
                    P.add("tensor", gfn, reads=[hfm[i] for i in range(HC)] + [wqpt[i] for i in range(HC)]
                          + [ucrows, ones], writes=[pbr])
                P.add("vector", lambda e, c=c:
                      e.tensor_scalar(out=f(FULL, NM2_O + c, 1), in0=f(FULL, MU2_O + c, 1),
                                      scalar1=f(FULL, RB_O + c, 1), scalar2=-1.0,
                                      op0=OP.mult, op1=OP.mult),
                      reads=[rb, statB], writes=[statB])
                P.add("vector", lambda e, c=c:
                      e.scalar_tensor_tensor(out=f(FULL, T3O_O, 1024), in0=f(FULL, UBC_O, 1024),
                                             scalar=f(FULL, NM2_O + c, 1), in1=f(FULL, X2TMB_O, 1024),
                                             op0=OP.mult, op1=OP.add),
                      reads=[ubc, x2tmB, statB], writes=[t3o])
                for half in range(2):
                    psr, pbr = (ps_c, pc) if half == 0 else (ps_d, pd)
                    P.add("vector", lambda e, c=c, half=half, psr=psr:
                          e.scalar_tensor_tensor(out=f(FULL, T3O_O + half * 512, 512),
                                                 in0=psr[:, 0:512], scalar=f(FULL, RB_O + c, 1),
                                                 in1=f(FULL, T3O_O + half * 512, 512),
                                                 op0=OP.mult, op1=OP.add),
                          reads=[pbr, rb, t3o], writes=[t3o])
                P.add("vector", lambda e: e.bn_stats(out=f(FULL, BNS_O, 6), in_=f(FULL, T3O_O, 512)),
                      reads=[t3o], writes=[statB])
                P.add("vector", lambda e: e.bn_stats(out=f(FULL, BNS_O + 6, 6),
                                                     in_=f(FULL, T3O_O + 512, 512)),
                      reads=[t3o], writes=[statB])
                P.add("vector", lambda e, c=c: e.bn_aggr(out=f(FULL, AGG_O, 2), in_=f(FULL, BNS_O, 12)),
                      reads=[statB], writes=[statB])
                P.add("vector", lambda e, c=c: e.tensor_copy(out=f(FULL, MU3_O + c, 1),
                                                             in_=f(FULL, AGG_O, 1)),
                      reads=[statB], writes=[statB])
                P.add("vector", lambda e, c=c: e.tensor_copy(out=f(FULL, VB_O + 4 + c, 1),
                                                             in_=f(FULL, AGG_O + 1, 1)),
                      reads=[statB], writes=[vb])
                dma("gpsimd", t3_s[cg * 128:(cg + 1) * 128, :], f(FULL, T3O_O, 1024),
                    reads=[t3o], writes=[t3s_bufs[cg]])
            if sc + 1 < n_sc:
                load_x2fm(sc + 1)

        P.add("scalar", lambda e: e.activation(out=f(FULL, RB_O + 4, 4), in_=f(FULL, VB_O + 4, 4),
                                               func=AF.Ln, bias=f(FULL, C_EPS, 1), scale=1.0),
              reads=[vb, consts], writes=[rb])
        P.add("scalar", lambda e: e.activation(out=f(FULL, RB_O + 4, 4), in_=f(FULL, RB_O + 4, 4),
                                               func=AF.Exp, bias=0.0, scale=f(FULL, C_NEGH, 1)),
              reads=[rb, consts], writes=[rb])
        for c in range(CPS):
            apply_ln3((n_sc - 1) * CPS + c, 4 + c)

        P.emit(block, sems)
    return nc


_NC_CACHE = {}


def _get_nc():
    if "nc" not in _NC_CACHE:
        _NC_CACHE["nc"] = build_core_program()
    return _NC_CACHE["nc"]


def kernel(L0_attn_x, L0_attn_W_attn, L0_attn_b_attn, L0_attn_W_proj, L0_attn_b_proj,
           L0_ffn_W_fc, L0_ffn_ln_w, L0_ffn_ln_b, L0_ffn_W_proj):
    x = np.ascontiguousarray(np.asarray(L0_attn_x, dtype=np.float32))
    wattn = np.ascontiguousarray(np.asarray(L0_attn_W_attn, dtype=np.float32))
    battn = np.asarray(L0_attn_b_attn, dtype=np.float32)
    wproj = np.ascontiguousarray(np.asarray(L0_attn_W_proj, dtype=np.float32))
    bproj = np.asarray(L0_attn_b_proj, dtype=np.float32)
    wfc = np.asarray(L0_ffn_W_fc, dtype=np.float32)
    lnw = np.asarray(L0_ffn_ln_w, dtype=np.float32)
    lnb = np.asarray(L0_ffn_ln_b, dtype=np.float32)
    wpf = np.asarray(L0_ffn_W_proj, dtype=np.float32)

    bqk = np.ascontiguousarray(battn[:2048].reshape(16, 128).T)
    bv = np.ascontiguousarray(battn[2048:].reshape(1, 1024))
    bproj_r = np.ascontiguousarray(bproj.reshape(1, 1024))
    wfct = np.ascontiguousarray(wfc.T)
    wpft = np.ascontiguousarray(wpf.T)
    lnw_l = np.ascontiguousarray(lnw.reshape(HC, 128).T)
    lnb_l = np.ascontiguousarray(lnb.reshape(HC, 128).T)

    in_maps = []
    for i in range(8):
        xc = x[2 * i:2 * i + 2]
        x_fm = np.ascontiguousarray(xc.transpose(0, 2, 1)).reshape(SEQS * D, T)
        x_tm = np.ascontiguousarray(xc.reshape(TOK, D))
        in_maps.append({
            "x_fm": x_fm, "x_tm": x_tm, "wattn": wattn, "bqk": bqk, "bv": bv,
            "bproj": bproj_r, "wproj": wproj, "wfct": wfct, "wpft": wpft,
            "lnw": lnw_l, "lnb": lnb_l,
        })

    nc = _get_nc()
    res = run_bass_kernel_spmd(nc, in_maps, core_ids=list(range(8)))
    out = np.concatenate([r["out"].reshape(2, T, D) for r in res.results], axis=0)
    return np.ascontiguousarray(out.astype(np.float32))
